# version 02
# baseline (speedup 1.0000x reference)
import contextlib
import numpy as np
import concourse.bass as bass
import concourse.mybir as mybir
from concourse.bass_utils import run_bass_kernel_spmd

F32 = mybir.dt.float32
BF16 = mybir.dt.bfloat16
AF = mybir.ActivationFunctionType
ALU = mybir.AluOpType
AX = mybir.AxisListType

T, D = 4096, 1024
NCH, CH = 8, 512
DI = 2048
CONVD = 2304
DPROJ = 4384
NH, HD, DS = 32, 64, 128
NQC, QC = 32, 128
EPS = 1e-5
NEG = -3.0e38


class Sem:
    def __init__(self, h):
        self.h = h
        self.n = 0
    def inc(self, k=1):
        self.n += k
        return self.n

class _RecInst:
    __slots__ = ("e",)
    def __init__(self, e):
        self.e = e
    def then_inc(self, sem, n):
        self.e[3].append((sem, n))
        return self

class Rec:
    def __init__(self):
        self.calls = []
    def __getattr__(self, name):
        calls = self.calls
        def call(*a, **k):
            e = (name, a, k, [])
            calls.append(e)
            return _RecInst(e)
        return call

def _replay(rec, eng):
    for name, a, k, incs in rec.calls:
        inst = getattr(eng, name)(*a, **k)
        for (sem, n) in incs:
            inst.then_inc(sem, n)

class Ctx:
    def __init__(self, nc, stack):
        self.nc = nc
        self.stack = stack
        self.scope = None
        self.scope_tiles = {}
        self.T = Rec()
        self.V = Rec()
        self.S = Rec()
        self.G = Rec()
        self.Y = Rec()
        self.sems = {}
        self.tiles = {}
        self.psums = {}
        self.nphase = 0
    def sem(self, name):
        if name not in self.sems:
            self.sems[name] = Sem(self.stack.enter_context(self.nc.semaphore(name)))
        return self.sems[name]
    def sbg(self, name, shape, dtype):
        if name not in self.tiles:
            self.tiles[name] = self.stack.enter_context(
                self.nc.sbuf_tensor(name, list(shape), dtype))
        return self.tiles[name]
    def sb(self, name, shape, dtype):
        if self.scope is not None and name in self.scope_tiles:
            return self.scope_tiles[name]
        if name in self.tiles:
            return self.tiles[name]
        if self.scope is None:
            return self.sbg(name, shape, dtype)
        self.scope_tiles[name] = self.scope.enter_context(
            self.nc.sbuf_tensor(f"p{self.nphase}_{name}", list(shape), dtype))
        return self.scope_tiles[name]
    def ps(self, name, shape, dtype=F32):
        if name not in self.psums:
            self.psums[name] = self.stack.enter_context(
                self.nc.psum_tensor(name, list(shape), dtype))
        return self.psums[name]
    def replay_into(self, block):
        block.tensor(lambda e: _replay(self.T, e))
        block.vector(lambda e: _replay(self.V, e))
        block.scalar(lambda e: _replay(self.S, e))
        block.gpsimd(lambda e: _replay(self.G, e))
        block.sync(lambda e: _replay(self.Y, e))
        self.T = Rec(); self.V = Rec(); self.S = Rec()
        self.G = Rec(); self.Y = Rec()
    def vedge(self, inst):
        vx = self.sem("vx")
        inst.then_inc(vx.h, 1)
        vx.inc()
        self.V.wait_ge(vx.h, vx.n)

    def finish(self):
        block = self.stack.enter_context(self.nc.Block())
        block.tensor(lambda e: _replay(self.T, e))
        block.vector(lambda e: _replay(self.V, e))
        block.scalar(lambda e: _replay(self.S, e))
        block.gpsimd(lambda e: _replay(self.G, e))
        block.sync(lambda e: _replay(self.Y, e))

ENGS = ("T", "V", "S", "G", "Y")

def barrier(ctx):
    for e in ENGS:
        s = ctx.sem("bar" + e)
        getattr(ctx, e).sem_inc(s.h, 1)
        s.inc()
    for e in ENGS:
        for o in ENGS:
            if o != e:
                so = ctx.sem("bar" + o)
                getattr(ctx, e).wait_ge(so.h, so.n)


def init_common(ctx):
    if "ones1" in ctx.tiles:
        return ctx.tiles["ones1"], ctx.tiles["onesc"]
    V = ctx.V
    ve = ctx.sem("ve")
    ones1 = ctx.sbg("ones1", [1, 128], F32)
    onesc = ctx.sbg("onesc", [128, 1], F32)
    eps1 = ctx.sbg("eps1", [1, 1], F32)
    one32 = ctx.sbg("one32", [32, 1], F32)
    V.memset(ones1[:, :], 1.0)
    V.memset(onesc[:, :], 1.0)
    V.memset(eps1[:, :], EPS)
    V.memset(one32[:, :], 1.0).then_inc(ve.h, 1)
    ve.inc()
    return ones1, onesc

def emit_layernorm_chunk(ctx, src, n_tiles, out_f32=None, out_bf16=None,
                         scale_col=None, bias_col=None, name="ln"):
    nd = n_tiles * 128
    Tq, V, S = ctx.T, ctx.V, ctx.S
    te, ve, se = ctx.sem("te"), ctx.sem("ve"), ctx.sem("se")
    onesc = ctx.tiles["onesc"]
    sq = ctx.sb("ln_sq", [128, 8, CH], F32)[:, 0:n_tiles, :]
    rows = ctx.sb("ln_rows", [1, 4, CH], F32)
    psS = ctx.ps("psA", [128, 2048], F32)

    S.wait_ge(ve.h, ve.n)
    S.activation(sq[:, :, :], src, AF.Square).then_inc(se.h, 1)
    se.inc()
    Tq.wait_ge(se.h, se.n)
    for j in range(n_tiles):
        Tq.matmul(psS[0:1, 0:CH], onesc[:, :], src[:, j, :],
                  start=(j == 0), stop=(j == n_tiles - 1))
    for j in range(n_tiles):
        i = Tq.matmul(psS[0:1, CH:2 * CH], onesc[:, :], sq[:, j, :],
                      start=(j == 0), stop=(j == n_tiles - 1))
    i.then_inc(te.h, 1)
    te.inc()
    V.wait_ge(te.h, te.n)
    V.tensor_scalar_mul(rows[0:1, 0, :], psS[0:1, 0:CH], 1.0 / nd)
    V.tensor_mul(rows[0:1, 2, :], rows[0:1, 0, :], rows[0:1, 0, :])
    V.tensor_scalar(rows[0:1, 1, :], psS[0:1, CH:2 * CH], 1.0 / nd, None,
                    ALU.mult)
    V.tensor_sub(rows[0:1, 1, :], rows[0:1, 1, :], rows[0:1, 2, :]).then_inc(ve.h, 1)
    ve.inc()
    S.wait_ge(ve.h, ve.n)
    S.activation(rows[0:1, 2, :], rows[0:1, 1, :], AF.Ln, bias=ctx.tiles["eps1"][0:1, 0:1])
    S.activation(rows[0:1, 1, :], rows[0:1, 2, :], AF.Exp, scale=-0.5).then_inc(se.h, 1)
    se.inc()
    ones1 = ctx.tiles["ones1"]
    Tq.wait_ge(se.h, se.n)
    Tq.matmul(psS[:, 1024:1024 + CH], ones1[:, :], rows[0:1, 0, :], start=True, stop=True)
    Tq.matmul(psS[:, 1536:1536 + CH], ones1[:, :], rows[0:1, 1, :],
              start=True, stop=True).then_inc(te.h, 1)
    te.inc()
    V.wait_ge(te.h, te.n)
    mb = psS[:, 1024:1024 + CH].unsqueeze(1).to_broadcast([128, n_tiles, CH])
    rb = psS[:, 1536:1536 + CH].unsqueeze(1).to_broadcast([128, n_tiles, CH])
    V.tensor_tensor(sq[:, :, :], src, mb, ALU.subtract)
    if scale_col is None and out_f32 is not None:
        V.tensor_tensor(out_f32, sq[:, :, :], rb, ALU.mult).then_inc(ve.h, 1)
        ve.inc()
        if out_bf16 is not None:
            V.tensor_copy(out_bf16, out_f32).then_inc(ve.h, 1)
            ve.inc()
    else:
        V.tensor_tensor(sq[:, :, :], sq[:, :, :], rb, ALU.mult).then_inc(ve.h, 1)
        ve.inc()
        S.wait_ge(ve.h, ve.n)
        for j in range(n_tiles):
            i = S.activation(out_bf16[:, j, :] if out_f32 is None else out_f32[:, j, :],
                             sq[:, j, :], AF.Identity,
                             bias=bias_col[:, j:j + 1], scale=scale_col[:, j:j + 1])
        i.then_inc(se.h, 1)
        se.inc()


def emit_phase_a(ctx, io):
    nc = ctx.nc
    Tq, V, S, G, Y = ctx.T, ctx.V, ctx.S, ctx.G, ctx.Y
    te, ve, se = ctx.sem("te"), ctx.sem("ve"), ctx.sem("se")
    dl, dy, gy = ctx.sem("dl"), ctx.sem("dy"), ctx.sem("gy")

    wip = ctx.sb("wip", [128, 8, DPROJ], BF16)
    lnw = ctx.sb("lnw", [128, 8], F32)
    lnb = ctx.sb("lnb", [128, 8], F32)
    cw = ctx.sb("cw", [128, 18, 4], F32)
    cb = ctx.sb("cb", [128, 18], F32)
    dtb = ctx.sb("dtb", [32, 1], F32)
    xbuf = ctx.sb("xbuf", [128, 8, CH], F32)
    xn0 = ctx.sb("xn0", [128, 8, CH], F32)
    xn0b = ctx.sb("xn0b", [128, 8, CH], BF16)
    xn1b = ctx.sb("xn1b", [128, 8, CH], BF16)
    szt = ctx.sb("szt", [128, 2, CH], F32)
    xbc = ctx.sb("xbc", [128, 18, 3 + CH], F32)
    cvo = ctx.sb("cvo", [128, 2, CH], BF16)
    cvt = ctx.sb("cvt", [128, CH], F32)
    dtw = ctx.sb("dtw", [32, 3, CH], F32)
    psB = ctx.ps("psB", [128, 2048], F32)

    G.dma_start(out=wip[:, :, :],
                in_=io["inprojT"].rearrange("(a p) o -> p a o", p=128)).then_inc(gy.h, 16)
    gy.inc(16)
    Y.dma_start(out=lnw[:, :], in_=io["lnw8"]).then_inc(dl.h, 16)
    Y.dma_start(out=lnb[:, :], in_=io["lnb8"]).then_inc(dl.h, 16)
    Y.dma_start(out=cw[:, :, :],
                in_=io["convw"].rearrange("(a p) j -> p a j", p=128)).then_inc(dl.h, 16)
    Y.dma_start(out=cb[:, :], in_=io["convb"]).then_inc(dl.h, 16)
    Y.dma_start(out=dtb[:, :], in_=io["dtbias"]).then_inc(dl.h, 16)
    dl.inc(16 * 5)
    V.wait_ge(dl.h, dl.n)
    V.memset(xbc[:, :, 0:3], 0.0).then_inc(ve.h, 1)
    ve.inc()
    init_common(ctx)
    Tq.wait_ge(gy.h, gy.n)

    for c in range(NCH):
        tok = slice(c * CH, (c + 1) * CH)
        Y.wait_ge(ve.h, ve.n)
        Y.wait_ge(se.h, se.n)
        Y.dma_start(out=xbuf[:, :, :],
                    in_=io["xT"][:, tok].rearrange("(a p) t -> p a t", p=128)
                    ).then_inc(dl.h, 16)
        dl.inc(16)
        for e in (V, S, Tq):
            e.wait_ge(dl.h, dl.n)
        emit_layernorm_chunk(ctx, xbuf[:, :, :], 8, out_f32=xn0[:, :, :],
                             out_bf16=xn0b[:, :, :], name="ln0")
        Y.wait_ge(ve.h, ve.n)
        Y.dma_start(out=io["xn0T"][:, tok].rearrange("(a p) t -> p a t", p=128),
                    in_=xn0[:, :, :]).then_inc(dy.h, 16)
        Y.dma_start(out=io["xn0Tb"][:, tok].rearrange("(a p) t -> p a t", p=128),
                    in_=xn0b[:, :, :]).then_inc(dy.h, 16)
        dy.inc(32)
        emit_layernorm_chunk(ctx, xn0[:, :, :], 8, out_f32=None, out_bf16=xn1b[:, :, :],
                             scale_col=lnw, bias_col=lnb, name="ln1")

        zdma = [0, 0]
        for o in range(35):
            m = 128 if o < 34 else 32
            bank = o % 2
            pdst = psB[0:m, bank * CH:(bank + 1) * CH]
            Tq.wait_ge(ve.h, ve.n)
            Tq.wait_ge(se.h, se.n)
            for kj in range(8):
                i = Tq.matmul(pdst, wip[:, kj, o * 128:o * 128 + m],
                              xn1b[:, kj, :], start=(kj == 0), stop=(kj == 7))
            i.then_inc(te.h, 1)
            te.inc()
            if o < 16:
                b2 = o % 2
                S.wait_ge(te.h, te.n)
                S.wait_ge(dy.h, zdma[b2])
                S.activation(szt[:, b2, :], pdst, AF.Silu).then_inc(se.h, 1)
                se.inc()
                Y.wait_ge(se.h, se.n)
                Y.dma_start(out=io["szT"][o * 128:(o + 1) * 128, tok],
                            in_=szt[:, b2, :]).then_inc(dy.h, 16)
                zdma[b2] = dy.inc(16)
            elif o < 34:
                V.wait_ge(te.h, te.n)
                V.tensor_copy(xbc[:, o - 16, 3:3 + CH], pdst).then_inc(ve.h, 1)
                ve.inc()
            else:
                V.wait_ge(te.h, te.n)
                V.tensor_copy(dtw[:, 0, :], pdst).then_inc(ve.h, 1)
                ve.inc()
                S.wait_ge(ve.h, ve.n)
                S.activation(dtw[:, 1, :], dtw[:, 0, :], AF.Abs, bias=dtb[:, 0:1])
                S.activation(dtw[:, 1, :], dtw[:, 1, :], AF.Exp, scale=-1.0)
                S.activation(dtw[:, 1, :], dtw[:, 1, :], AF.Ln,
                             bias=ctx.tiles["one32"][:, 0:1])
                S.activation(dtw[:, 2, :], dtw[:, 0, :], AF.Relu,
                             bias=dtb[:, 0:1]).then_inc(se.h, 1)
                se.inc()
                V.wait_ge(se.h, se.n)
                V.tensor_add(dtw[:, 0, :], dtw[:, 1, :], dtw[:, 2, :]).then_inc(ve.h, 1)
                ve.inc()
                Y.wait_ge(ve.h, ve.n)
                Y.dma_start(out=io["dtT"][:, tok], in_=dtw[:, 0, :]).then_inc(dy.h, 16)
                dy.inc(16)
        cdma = [0, 0]
        for a in range(18):
            b2 = a % 2
            V.wait_ge(ve.h, ve.n)
            V.tensor_scalar_mul(cvt[:, :], xbc[:, a, 0:CH], cw[:, a, 0:1])
            for j in range(1, 4):
                i = V.scalar_tensor_tensor(cvt[:, :], xbc[:, a, j:j + CH],
                                           cw[:, a, j:j + 1], cvt[:, :],
                                           ALU.mult, ALU.add)
            i.then_inc(ve.h, 1)
            ve.inc()
            S.wait_ge(ve.h, ve.n)
            S.wait_ge(dy.h, cdma[b2])
            S.activation(cvo[:, b2, :], cvt[:, :], AF.Silu,
                         bias=cb[:, a:a + 1]).then_inc(se.h, 1)
            se.inc()
            Y.wait_ge(se.h, se.n)
            Y.dma_start(out=io["convoT"][a * 128:(a + 1) * 128, tok],
                        in_=cvo[:, b2, :]).then_inc(dy.h, 16)
            cdma[b2] = dy.inc(16)
            V.wait_ge(se.h, se.n)
            V.tensor_copy(xbc[:, a, 0:3], xbc[:, a, CH:CH + 3]).then_inc(ve.h, 1)
            ve.inc()
    Y.wait_ge(dy.h, dy.n)



def emit_phase_b(ctx, io):
    Tq, V, S, G, Y = ctx.T, ctx.V, ctx.S, ctx.G, ctx.Y
    te, ve, se = ctx.sem("te"), ctx.sem("ve"), ctx.sem("se")
    dl, dy, gy = ctx.sem("dl"), ctx.sem("dy"), ctx.sem("gy")

    dtf = ctx.sb("dtf", [32, T], F32)
    alog = ctx.sb("alog_s", [32, 1], F32)
    acol = ctx.sb("acol", [32, 1], F32)
    drep = ctx.sb("drep_s", [128, 16], F32)
    identf = ctx.sb("identf_s", [128, 128], F32)
    identb = ctx.sb("identb", [128, 128], BF16)
    maskneg = ctx.sb("maskneg_s", [128, 128], F32)
    rsel = ctx.sb("rsel", [32, NH * 128], F32)
    ones32 = ctx.sb("ones32", [32, 128], F32)
    zer32 = ctx.sb("zer32", [32, QC], F32)
    xcb = ctx.sb("xcb", [128, 16, QC], BF16)
    Bc = ctx.sb("Bc", [128, QC], BF16)
    Cc = ctx.sb("Cc", [128, QC], BF16)
    acm = ctx.sb("acm", [32, QC], F32)
    cum = ctx.sb("cum", [32, QC], F32)
    negc = ctx.sb("negc", [32, QC], F32)
    ecum = ctx.sb("ecum", [32, QC], F32)
    diagc = ctx.sb("diagc", [32, 32], F32)
    cumT = ctx.sb("cumT", [128, 32], F32)
    dt_tm = ctx.sb("dt_tm", [128, 32], F32)
    cendb = ctx.sb("cendb", [128, 32], F32)
    decb = ctx.sb("decb", [128, 32], F32)
    wfac = ctx.sb("wfac", [128, 32], F32)
    x_tm = ctx.sb("x_tm", [128, DI], BF16)
    dtx = ctx.sb("dtx", [128, DI], BF16)
    wx = ctx.sb("wx", [128, DI], BF16)
    B_tm = ctx.sb("B_tm", [128, QC], BF16)
    g_sb = ctx.sb("g_sb", [128, QC], F32)
    dm_sb = ctx.sb("dm_sb", [128, 512], F32)
    dm2 = ctx.sb("dm2", [128, 512], F32)
    Stb = ctx.sb("Stb", [128, 512], BF16)
    St2b = ctx.sb("St2b", [128, 512], BF16)
    hT = ctx.sb("hT", [128, DI], F32)
    hTb = ctx.sb("hTb", [128, DI], BF16)
    ych = ctx.sb("ych", [128, 16, QC], F32)
    psA = ctx.ps("psA", [128, 2048], F32)
    psB = ctx.ps("psB", [128, 2048], F32)

    Y.dma_start(out=dtf[:, :], in_=io["dtT"]).then_inc(dl.h, 16)
    Y.dma_start(out=alog[:, :], in_=io["alog"]).then_inc(dl.h, 16)
    Y.dma_start(out=drep[:, :], in_=io["drep"]).then_inc(dl.h, 16)
    Y.dma_start(out=identf[:, :], in_=io["identf"]).then_inc(dl.h, 16)
    Y.dma_start(out=maskneg[:, :], in_=io["maskneg"]).then_inc(dl.h, 16)
    Y.dma_start(out=rsel[:, :], in_=io["rowsel32"]).then_inc(dl.h, 16)
    dl.inc(16 * 6)
    G.dma_start(out=identb[:, :], in_=io["identf"]).then_inc(gy.h, 16)
    gy.inc(16)
    V.wait_ge(dl.h, dl.n)
    V.wait_ge(gy.h, gy.n)
    V.memset(ones32[:, :], 1.0)
    V.memset(zer32[:, :], 0.0)
    V.memset(hT[:, :], 0.0)
    V.memset(hTb[:, :], 0.0).then_inc(ve.h, 1)
    ve.inc()
    S.wait_ge(ve.h, ve.n)
    S.activation(acol[:, :], alog[:, :], AF.Exp).then_inc(se.h, 1)
    se.inc()
    V.wait_ge(se.h, se.n)
    ctx.vedge(V.tensor_sub(acol[:, :], zer32[:, 0:1], acol[:, :]))
    V.sem_inc(ve.h, 1)
    ve.inc()

    id32 = identf[0:32, 0:32]
    for c in range(NQC):
        tok = slice(c * QC, (c + 1) * QC)
        Y.wait_ge(ve.h, ve.n)
        Y.wait_ge(te.h, te.n)
        Y.dma_start(out=xcb[:, :, :],
                    in_=io["convoT"][0:DI, tok].rearrange("(a p) t -> p a t", p=128)
                    ).then_inc(dl.h, 16)
        Y.dma_start(out=Bc[:, :], in_=io["convoT"][DI:DI + 128, tok]).then_inc(dl.h, 16)
        Y.dma_start(out=Cc[:, :], in_=io["convoT"][DI + 128:DI + 256, tok]
                    ).then_inc(dl.h, 16)
        dl.inc(48)
        V.wait_ge(dl.h, dl.n)
        V.tensor_scalar_mul(acm[:, :], dtf[:, tok], acol[:, 0:1])
        V.tensor_tensor_scan(cum[:, :], acm[:, :], acm[:, :], 0.0, ALU.add, ALU.bypass)
        ctx.vedge(V.tensor_sub(negc[:, :], zer32[:, :], cum[:, :]))
        V.tensor_mul(diagc[:, :], id32, cum[:, QC - 1:QC].to_broadcast([32, 32])
                     ).then_inc(ve.h, 1)
        ve.inc()
        S.wait_ge(ve.h, ve.n)
        S.activation(ecum[:, :], cum[:, :], AF.Exp).then_inc(se.h, 1)
        se.inc()
        b2 = 1024
        Tq.wait_ge(ve.h, ve.n)
        Tq.wait_ge(dl.h, dl.n)
        Tq.transpose(psB[:, b2 + 0:b2 + 32], cum[:, :], id32)
        Tq.transpose(psB[:, b2 + 32:b2 + 64], dtf[:, tok], id32)
        Tq.matmul(psB[:, b2 + 64:b2 + 96], ones32[:, :], diagc[:, :],
                  start=True, stop=True)
        Tq.matmul(psB[:, b2 + 160:b2 + 288], Bc[:, :], Cc[:, :], start=True, stop=True)
        Tq.transpose(psB[:, b2 + 96:b2 + 160].bitcast(BF16), Bc[:, :],
                     identb[:, :]).then_inc(te.h, 1)
        te.inc()
        V.wait_ge(te.h, te.n)
        V.tensor_copy(cumT[:, :], psB[:, b2 + 0:b2 + 32])
        V.tensor_copy(dt_tm[:, :], psB[:, b2 + 32:b2 + 64])
        V.tensor_copy(cendb[:, :], psB[:, b2 + 64:b2 + 96])
        V.tensor_copy(g_sb[:, :], psB[:, b2 + 160:b2 + 288])
        V.tensor_copy(B_tm[:, :], psB[:, b2 + 96:b2 + 160].bitcast(BF16)
                      ).then_inc(ve.h, 1)
        ve.inc()
        S.wait_ge(ve.h, ve.n)
        S.activation(decb[:, :], cendb[:, :], AF.Exp).then_inc(se.h, 1)
        se.inc()
        V.tensor_sub(wfac[:, :], cendb[:, :], cumT[:, :]).then_inc(ve.h, 1)
        ve.inc()
        S.wait_ge(ve.h, ve.n)
        S.activation(wfac[:, :], wfac[:, :], AF.Exp).then_inc(se.h, 1)
        se.inc()
        V.wait_ge(se.h, se.n)
        ctx.vedge(V.tensor_mul(wfac[:, :], wfac[:, :], dt_tm[:, :]))
        V.sem_inc(ve.h, 1)
        ve.inc()
        b3 = 1536
        for r in range(2):
            Tq.wait_ge(ve.h, ve.n)
            for j in range(8):
                i = Tq.transpose(psB[:, b3 + j * 64:b3 + (j + 1) * 64].bitcast(BF16),
                                 xcb[:, r * 8 + j, :], identb[:, :])
            i.then_inc(te.h, 1)
            te.inc()
            V.wait_ge(te.h, te.n)
            for j in range(8):
                i = V.tensor_copy(x_tm[:, (r * 8 + j) * 128:(r * 8 + j + 1) * 128],
                                  psB[:, b3 + j * 64:b3 + (j + 1) * 64].bitcast(BF16))
            i.then_inc(ve.h, 1)
            ve.inc()
        x3 = x_tm[:, :].rearrange("p (h q) -> p h q", q=HD)
        V.tensor_tensor(dtx[:, :].rearrange("p (h q) -> p h q", q=HD), x3,
                        dt_tm[:, :].unsqueeze(2).to_broadcast([128, NH, HD]), ALU.mult)
        V.tensor_tensor(wx[:, :].rearrange("p (h q) -> p h q", q=HD), x3,
                        wfac[:, :].unsqueeze(2).to_broadcast([128, NH, HD]),
                        ALU.mult).then_inc(ve.h, 1)
        ve.inc()
        for g in range(8):
            heads = range(4 * g, 4 * g + 4)
            Tq.wait_ge(ve.h, ve.n)
            for j, h in enumerate(heads):
                Tq.matmul(psB[:, j * 128:(j + 1) * 128],
                          rsel[:, h * 128:(h + 1) * 128], cum[:, :],
                          start=(j == 0), stop=False)
            Tq.matmul(psB[:, 0:512], negc[:, :], rsel[:, 4 * g * 128:4 * g * 128 + 512],
                      start=False, stop=True).then_inc(te.h, 1)
            te.inc()
            Tq.wait_ge(se.h, se.n)
            for j, h in enumerate(heads):
                i = Tq.matmul(psB[:, 512 + j * 128:512 + (j + 1) * 128],
                              rsel[:, h * 128:(h + 1) * 128], ecum[:, :],
                              start=(j == 0), stop=(j == 3))
            i.then_inc(te.h, 1)
            te.inc()
            V.wait_ge(te.h, te.n)
            V.tensor_tensor(dm_sb[:, :].rearrange("p (r n) -> p r n", r=4),
                            psB[:, 0:512].rearrange("p (r n) -> p r n", r=4),
                            maskneg[:, :].unsqueeze(1).to_broadcast([128, 4, 128]),
                            ALU.add).then_inc(ve.h, 1)
            ve.inc()
            S.wait_ge(ve.h, ve.n)
            S.activation(dm2[:, :], dm_sb[:, :], AF.Exp).then_inc(se.h, 1)
            se.inc()
            V.wait_ge(se.h, se.n)
            V.tensor_tensor(Stb[:, :].rearrange("p (r n) -> p r n", r=4),
                            dm2[:, :].rearrange("p (r n) -> p r n", r=4),
                            g_sb[:, :].unsqueeze(1).to_broadcast([128, 4, 128]),
                            ALU.mult)
            V.tensor_tensor(St2b[:, :].rearrange("p (r n) -> p r n", r=4),
                            psB[:, 512:1024].rearrange("p (r n) -> p r n", r=4),
                            Cc[:, :].unsqueeze(1).to_broadcast([128, 4, 128]),
                            ALU.mult).then_inc(ve.h, 1)
            ve.inc()
            Tq.wait_ge(ve.h, ve.n)
            for j, h in enumerate(heads):
                slot = h % 16
                out = psA[0:64, slot * 128:(slot + 1) * 128]
                Tq.matmul(out, dtx[:, h * HD:(h + 1) * HD], Stb[:, j * 128:(j + 1) * 128],
                          start=(j == 0), stop=False, skip_group_check=True)
                i = Tq.matmul(out, hTb[:, h * HD:(h + 1) * HD],
                              St2b[:, j * 128:(j + 1) * 128],
                              start=False, stop=(j == 3), skip_group_check=True)
            i.then_inc(te.h, 1)
            te.inc()
            V.wait_ge(te.h, te.n)
            for j, h in enumerate(heads):
                slot = h % 16
                pair, half = h // 2, h % 2
                rows = slice(half * 64, (half + 1) * 64)
                i = V.scalar_tensor_tensor(ych[rows, pair, :], xcb[rows, pair, :],
                                           drep[rows, pair:pair + 1],
                                           psA[0:64, slot * 128:(slot + 1) * 128],
                                           ALU.mult, ALU.add)
            i.then_inc(ve.h, 1)
            ve.inc()
        Tq.wait_ge(ve.h, ve.n)
        for q in range(4):
            i = Tq.matmul(psB[:, q * 512:(q + 1) * 512], B_tm[:, :],
                          wx[:, q * 512:(q + 1) * 512], start=True, stop=True,
                          skip_group_check=True)
        i.then_inc(te.h, 1)
        te.inc()
        V.wait_ge(te.h, te.n)
        V.tensor_tensor(hT[:, :].rearrange("p (h q) -> p h q", q=HD),
                        hT[:, :].rearrange("p (h q) -> p h q", q=HD),
                        decb[:, :].unsqueeze(2).to_broadcast([128, NH, HD]), ALU.mult)
        V.tensor_add(hT[:, :], hT[:, :], psB[:, :])
        V.tensor_copy(hTb[:, :], hT[:, :]).then_inc(ve.h, 1)
        ve.inc()
        Y.wait_ge(ve.h, ve.n)
        Y.dma_start(out=io["yT"][:, tok].rearrange("(a p) t -> p a t", p=128),
                    in_=ych[:, :, :]).then_inc(dy.h, 16)
        dy.inc(16)
        if c == 0 and "dbg_cum" in io:
            for nm, tile in (("dbg_cum", cum), ("dbg_g", g_sb), ("dbg_stb", Stb),
                             ("dbg_st2b", St2b), ("dbg_dtx", dtx), ("dbg_hT", hT),
                             ("dbg_xtm", x_tm), ("dbg_wfac", wfac),
                             ("dbg_cumT", cumT), ("dbg_dttm", dt_tm),
                             ("dbg_btm", B_tm)):
                Y.dma_start(out=io[nm], in_=tile[:, :]).then_inc(dy.h, 16)
                dy.inc(16)
    Y.wait_ge(dy.h, dy.n)


def build_phase_b(dbg=False):
    nc = bass.Bass("TRN2", target_bir_lowering=False)
    io = {}
    if dbg:
        for nm, shape, dt_ in (("dbg_cum", [32, QC], F32), ("dbg_g", [128, QC], F32),
                               ("dbg_stb", [128, 512], BF16), ("dbg_st2b", [128, 512], BF16),
                               ("dbg_dtx", [128, DI], BF16), ("dbg_hT", [128, DI], F32),
                               ("dbg_xtm", [128, DI], BF16), ("dbg_wfac", [128, 32], F32),
                               ("dbg_cumT", [128, 32], F32), ("dbg_dttm", [128, 32], F32),
                               ("dbg_btm", [128, QC], BF16)):
            io[nm] = nc.dram_tensor(nm, shape, dt_, kind="ExternalOutput").ap()
    io["convoT"] = nc.dram_tensor("convoT", [CONVD, T], BF16, kind="ExternalInput").ap()
    io["dtT"] = nc.dram_tensor("dtT", [32, T], F32, kind="ExternalInput").ap()
    io["alog"] = nc.dram_tensor("alog", [32, 1], F32, kind="ExternalInput").ap()
    io["drep"] = nc.dram_tensor("drep", [128, 16], F32, kind="ExternalInput").ap()
    io["identf"] = nc.dram_tensor("identf", [128, 128], F32, kind="ExternalInput").ap()
    io["maskneg"] = nc.dram_tensor("maskneg", [128, 128], F32, kind="ExternalInput").ap()
    io["rowsel32"] = nc.dram_tensor("rowsel32", [32, NH * 128], F32,
                                    kind="ExternalInput").ap()
    io["yT"] = nc.dram_tensor("yT", [DI, T], F32, kind="ExternalOutput").ap()
    with contextlib.ExitStack() as stack:
        ctx = Ctx(nc, stack)
        emit_phase_b(ctx, io)
        ctx.finish()
    return nc

def prep_consts():
    s = np.arange(128)
    consts = {}
    consts["identf"] = np.eye(128, dtype=np.float32)
    consts["maskneg"] = np.where(s[:, None] <= s[None, :], 0.0, NEG).astype(np.float32)
    rsel = np.zeros((32, NH * 128), np.float32)
    for h in range(NH):
        rsel[h, h * 128:(h + 1) * 128] = 1.0
    consts["rowsel32"] = rsel
    sel16 = np.zeros((16, 16 * 128), np.float32)
    for e in range(16):
        sel16[e, e * 128:(e + 1) * 128] = 1.0
    consts["sel16"] = sel16
    consts["sel8"] = np.ascontiguousarray(sel16[:8, :8 * 128])
    return consts


def emit_phase_c(ctx, io):
    Tq, V, S, G, Y = ctx.T, ctx.V, ctx.S, ctx.G, ctx.Y
    te, ve, se = ctx.sem("te"), ctx.sem("ve"), ctx.sem("se")
    dl, dy, gy = ctx.sem("dl"), ctx.sem("dy"), ctx.sem("gy")

    wop = ctx.sb("wop", [128, 16, D], BF16)
    nw = ctx.sb("nw", [128, 16], F32)
    ybuf = ctx.sb("ybuf", [128, 16, CH], F32)
    zbuf = ctx.sb("zbuf", [128, 16, CH], F32)
    xnbuf = ctx.sb("xnbuf", [128, 8, CH], F32)
    y3b = ctx.sb("y3b", [128, 16, CH], BF16)
    s0c = ctx.sb("s0c", [128, 8, CH], F32)
    rows = ctx.sb("c_rows", [1, 2, CH], F32)
    psA = ctx.ps("psA", [128, 2048], F32)
    psB = ctx.ps("psB", [128, 2048], F32)

    G.dma_start(out=wop[:, :, :],
                in_=io["outprojT"].rearrange("(a p) o -> p a o", p=128)).then_inc(gy.h, 16)
    gy.inc(16)
    Y.dma_start(out=nw[:, :], in_=io["normw16"]).then_inc(dl.h, 16)
    dl.inc(16)
    init_common(ctx)
    V.wait_ge(dl.h, dl.n)
    Tq.wait_ge(gy.h, gy.n)
    onesc = ctx.tiles["onesc"]
    ones1 = ctx.tiles["ones1"]

    for c in range(NCH):
        tok = slice(c * CH, (c + 1) * CH)
        Y.wait_ge(ve.h, ve.n)
        Y.wait_ge(se.h, se.n)
        Y.dma_start(out=ybuf[:, :, :],
                    in_=io["yT"][:, tok].rearrange("(a p) t -> p a t", p=128)
                    ).then_inc(dl.h, 16)
        Y.dma_start(out=zbuf[:, :, :],
                    in_=io["szT"][:, tok].rearrange("(a p) t -> p a t", p=128)
                    ).then_inc(dl.h, 16)
        Y.dma_start(out=xnbuf[:, :, :],
                    in_=io["xn0T"][:, tok].rearrange("(a p) t -> p a t", p=128)
                    ).then_inc(dl.h, 16)
        dl.inc(48)
        V.wait_ge(dl.h, dl.n)
        V.tensor_mul(ybuf[:, :, :], ybuf[:, :, :], zbuf[:, :, :]).then_inc(ve.h, 1)
        ve.inc()
        S.wait_ge(ve.h, ve.n)
        S.activation(zbuf[:, :, :], ybuf[:, :, :], AF.Square).then_inc(se.h, 1)
        se.inc()
        Tq.wait_ge(se.h, se.n)
        for j in range(16):
            i = Tq.matmul(psA[0:1, 0:CH], onesc[:, :], zbuf[:, j, :],
                          start=(j == 0), stop=(j == 15))
        i.then_inc(te.h, 1)
        te.inc()
        V.wait_ge(te.h, te.n)
        V.tensor_scalar_mul(rows[0:1, 0, :], psA[0:1, 0:CH], 1.0 / DI).then_inc(ve.h, 1)
        ve.inc()
        S.wait_ge(ve.h, ve.n)
        S.activation(rows[0:1, 1, :], rows[0:1, 0, :], AF.Ln,
                     bias=ctx.tiles["eps1"][0:1, 0:1])
        S.activation(rows[0:1, 0, :], rows[0:1, 1, :], AF.Exp, scale=-0.5).then_inc(se.h, 1)
        se.inc()
        Tq.wait_ge(se.h, se.n)
        Tq.matmul(psA[:, 1024:1024 + CH], ones1[:, :], rows[0:1, 0, :],
                  start=True, stop=True).then_inc(te.h, 1)
        te.inc()
        V.wait_ge(te.h, te.n)
        V.tensor_tensor(ybuf[:, :, :], ybuf[:, :, :],
                        psA[:, 1024:1024 + CH].unsqueeze(1).to_broadcast([128, 16, CH]),
                        ALU.mult).then_inc(ve.h, 1)
        ve.inc()
        S.wait_ge(ve.h, ve.n)
        for j in range(16):
            i = S.activation(y3b[:, j, :], ybuf[:, j, :], AF.Identity,
                             scale=nw[:, j:j + 1])
        i.then_inc(se.h, 1)
        se.inc()
        Tq.wait_ge(se.h, se.n)
        for o in range(8):
            bank = o % 2
            pdst = psB[:, bank * CH:(bank + 1) * CH]
            Tq.wait_ge(ve.h, ve.n)
            for kj in range(16):
                i = Tq.matmul(pdst, wop[:, kj, o * 128:(o + 1) * 128], y3b[:, kj, :],
                              start=(kj == 0), stop=(kj == 15))
            i.then_inc(te.h, 1)
            te.inc()
            V.wait_ge(te.h, te.n)
            V.tensor_add(s0c[:, o, :], xnbuf[:, o, :], pdst).then_inc(ve.h, 1)
            ve.inc()
        Y.wait_ge(ve.h, ve.n)
        Y.dma_start(out=io["s0T"][:, tok].rearrange("(a p) t -> p a t", p=128),
                    in_=s0c[:, :, :]).then_inc(dy.h, 16)
        dy.inc(16)
    Y.wait_ge(dy.h, dy.n)


def build_phase_c():
    nc = bass.Bass("TRN2", target_bir_lowering=False)
    io = {}
    io["yT"] = nc.dram_tensor("yT", [DI, T], F32, kind="ExternalInput").ap()
    io["szT"] = nc.dram_tensor("szT", [DI, T], F32, kind="ExternalInput").ap()
    io["xn0T"] = nc.dram_tensor("xn0T", [D, T], F32, kind="ExternalInput").ap()
    io["outprojT"] = nc.dram_tensor("outprojT", [DI, D], F32, kind="ExternalInput").ap()
    io["normw16"] = nc.dram_tensor("normw16", [128, 16], F32, kind="ExternalInput").ap()
    io["s0T"] = nc.dram_tensor("s0T", [D, T], F32, kind="ExternalOutput").ap()
    with contextlib.ExitStack() as stack:
        ctx = Ctx(nc, stack)
        emit_phase_c(ctx, io)
        ctx.finish()
    return nc


def emit_phase_d(ctx, io):
    Tq, V, S, G, Y = ctx.T, ctx.V, ctx.S, ctx.G, ctx.Y
    te, ve, se = ctx.sem("te"), ctx.sem("ve"), ctx.sem("se")
    dl, dy, gy = ctx.sem("dl"), ctx.sem("dy"), ctx.sem("gy")

    g0w = ctx.sb("g0w", [128, 8, 16], F32)
    g0b = ctx.sb("g0b", [16, 1], F32)
    sel16 = ctx.sb("sel16_s", [16, 16 * 128], F32)
    identf = ctx.sb("identf_s", [128, 128], F32)
    xnb = ctx.sb("xnb", [128, 8, CH], BF16)
    xnf = ctx.sb("xnf", [128, 8, CH], F32)
    xrbuf = ctx.sb("xrbuf", [128, 8, CH], F32)
    s0buf = ctx.sb("s0buf", [128, 8, CH], F32)
    y0acc = ctx.sb("y0acc", [128, 8, CH], F32)
    w13s_ = ctx.sb("w13s_", [128, 2, 8, D], BF16)
    w2s_ = ctx.sb("w2s_", [128, 2, 4, D], BF16)
    sct = ctx.sb("sct", [16, 4, CH], F32)
    s_tm = ctx.sb("s_tm", [128, 4, 16], F32)
    b_tm = ctx.sb("b_tm", [128, 4, 16], F32)
    msk = ctx.sb("msk", [128, 4, 16], F32)
    selm = ctx.sb("selm", [128, 4, 16], F32)
    rcol = ctx.sb("rcol", [128, 4, 1], F32)
    combT = ctx.sb("combT", [16, CH], F32)
    bcomb = ctx.sb("bcomb", [128, CH], F32)
    sgt = ctx.sb("sgt", [128, CH], F32)
    sgb = ctx.sb("sgb", [128, 4, CH], BF16)
    x1c = ctx.sb("x1c", [128, 8, CH], F32)
    psA = ctx.ps("psA", [128, 2048], F32)
    psB = ctx.ps("psB", [128, 2048], F32)

    Y.dma_start(out=g0w[:, :, :],
                in_=io["gate0T"].rearrange("(a p) e -> p a e", p=128)).then_inc(dl.h, 16)
    dl.inc(16)
    Y.dma_start(out=g0b[:, :], in_=io["gate0b"]).then_inc(dl.h, 16)
    Y.dma_start(out=sel16[:, :], in_=io["sel16"]).then_inc(dl.h, 16)
    Y.dma_start(out=identf[:, :], in_=io["identf"]).then_inc(dl.h, 16)
    dl.inc(16 * 3)
    init_common(ctx)
    V.wait_ge(dl.h, dl.n)

    def load_exp(e, buf):
        if e < 16:
            Y.dma_start(out=w13s_[:, buf, :, :],
                        in_=io["w13b"][e].rearrange("(a p) o -> p a o", p=128)
                        ).then_inc(dl.h, 16)
            Y.dma_start(out=w2s_[:, buf, :, :],
                        in_=io["w20b"][e].rearrange("(a p) o -> p a o", p=128)
                        ).then_inc(dl.h, 16)
        else:
            Y.dma_start(out=w13s_[:, buf, :, :],
                        in_=io["w13sb"].rearrange("(a p) o -> p a o", p=128)
                        ).then_inc(dl.h, 16)
            Y.dma_start(out=w2s_[:, buf, :, :],
                        in_=io["w2sb"].rearrange("(a p) o -> p a o", p=128)
                        ).then_inc(dl.h, 16)
        return dl.inc(32)

    id16 = identf[0:16, 0:16]
    for c in range(NCH):
        tok = slice(c * CH, (c + 1) * CH)
        Y.wait_ge(ve.h, ve.n)
        Y.wait_ge(se.h, se.n)
        Y.dma_start(out=xnb[:, :, :],
                    in_=io["xn0Tb"][:, tok].rearrange("(a p) t -> p a t", p=128)
                    ).then_inc(dl.h, 16)
        Y.dma_start(out=xnf[:, :, :],
                    in_=io["xn0T"][:, tok].rearrange("(a p) t -> p a t", p=128)
                    ).then_inc(dl.h, 16)
        Y.dma_start(out=xrbuf[:, :, :],
                    in_=io["xT"][:, tok].rearrange("(a p) t -> p a t", p=128)
                    ).then_inc(dl.h, 16)
        Y.dma_start(out=s0buf[:, :, :],
                    in_=io["s0T"][:, tok].rearrange("(a p) t -> p a t", p=128)
                    ).then_inc(dl.h, 16)
        dl.inc(64)
        ldtoks = {0: load_exp(0, 0)}
        Tq.wait_ge(dl.h, dl.n - 32)
        Tq.wait_ge(ve.h, ve.n)
        for kj in range(8):
            i = Tq.matmul(psA[0:16, 0:CH], g0w[:, kj, :], xnf[:, kj, :],
                          start=(kj == 0), stop=(kj == 7))
        i.then_inc(te.h, 1)
        te.inc()
        S.wait_ge(te.h, te.n)
        S.activation(sct[:, 0, :], psA[0:16, 0:CH], AF.Sigmoid).then_inc(se.h, 1)
        se.inc()
        V.wait_ge(se.h, se.n)
        V.tensor_scalar_add(sct[:, 1, :], sct[:, 0, :], g0b[:, 0:1]).then_inc(ve.h, 1)
        ve.inc()
        Tq.wait_ge(ve.h, ve.n)
        for q in range(4):
            Tq.transpose(psB[:, q * 16:(q + 1) * 16], sct[:, 0, q * 128:(q + 1) * 128],
                         id16)
            i = Tq.transpose(psB[:, 64 + q * 16:64 + (q + 1) * 16],
                             sct[:, 1, q * 128:(q + 1) * 128], id16)
        i.then_inc(te.h, 1)
        te.inc()
        V.wait_ge(te.h, te.n)
        V.tensor_copy(s_tm[:, :, :], psB[:, 0:64].rearrange("p (q e) -> p q e", e=16))
        V.tensor_copy(b_tm[:, :, :], psB[:, 64:128].rearrange("p (q e) -> p q e", e=16))
        V.tensor_copy(msk[:, :, :], b_tm[:, :, :])
        V.memset(selm[:, :, :], 0.0)
        for r in range(4):
            for q in range(4):
                ctx.vedge(V.tensor_reduce(rcol[:, q, :], msk[:, q, :], AX.X, ALU.max))
                V.tensor_scalar(msk[:, q, :], msk[:, q, :], rcol[:, q, 0:1], None,
                                ALU.is_ge)
            V.tensor_add(selm[:, :, :], selm[:, :, :], msk[:, :, :])
            if r < 3:
                V.scalar_tensor_tensor(msk[:, :, :], selm[:, :, :], -1e30,
                                       b_tm[:, :, :], ALU.mult, ALU.add)
        V.tensor_scalar_min(selm[:, :, :], selm[:, :, :], 1.0)
        V.tensor_mul(selm[:, :, :], selm[:, :, :], s_tm[:, :, :])
        for q in range(4):
            ctx.vedge(V.tensor_reduce(rcol[:, q, :], selm[:, q, :], AX.X, ALU.add))
            ctx.vedge(V.reciprocal(rcol[:, q, :], rcol[:, q, :]))
            i = V.tensor_scalar(selm[:, q, :], selm[:, q, :], rcol[:, q, 0:1], 2.5,
                                ALU.mult, ALU.mult)
        i.then_inc(ve.h, 1)
        ve.inc()
        Tq.wait_ge(ve.h, ve.n)
        for q in range(4):
            i = Tq.transpose(psB[0:16, 256 + q * 128:256 + (q + 1) * 128],
                             selm[:, q, :], identf[:, :])
        i.then_inc(te.h, 1)
        te.inc()
        V.wait_ge(te.h, te.n)
        for q in range(4):
            i = V.tensor_copy(combT[:, q * 128:(q + 1) * 128],
                              psB[0:16, 256 + q * 128:256 + (q + 1) * 128])
        i.then_inc(ve.h, 1)
        ve.inc()
        if c == 0 and "dbg_sct" in io:
            Y.wait_ge(ve.h, ve.n)
            Y.dma_start(out=io["dbg_sct"], in_=sct[:, :, :]).then_inc(dy.h, 16)
            Y.dma_start(out=io["dbg_stm"], in_=s_tm[:, :, :]).then_inc(dy.h, 16)
            Y.dma_start(out=io["dbg_btm2"], in_=b_tm[:, :, :]).then_inc(dy.h, 16)
            Y.dma_start(out=io["dbg_selm"], in_=selm[:, :, :]).then_inc(dy.h, 16)
            Y.dma_start(out=io["dbg_combT"], in_=combT[:, :]).then_inc(dy.h, 16)
            dy.inc(80)
        V.wait_ge(dl.h, dl.n)
        V.memset(y0acc[:, :, :], 0.0).then_inc(ve.h, 1)
        ve.inc()
        for e in range(17):
            buf = e % 2
            if e + 1 < 17:
                Y.wait_ge(te.h, te.n)
                ldtoks[e + 1] = load_exp(e + 1, 1 - buf)
            if e < 16:
                Tq.wait_ge(ve.h, ve.n)
                Tq.matmul(psA[:, 512:512 + CH], sel16[:, e * 128:(e + 1) * 128],
                          combT[:, :], start=True, stop=True).then_inc(te.h, 1)
                te.inc()
                S.wait_ge(te.h, te.n)
                S.activation(bcomb[:, :], psA[:, 512:512 + CH],
                             AF.Identity).then_inc(se.h, 1)
                se.inc()
            Tq.wait_ge(dl.h, ldtoks[e])
            for j in range(4):
                Tq.wait_ge(se.h, se.n)
                Tq.wait_ge(ve.h, ve.n)
                for kj in range(8):
                    Tq.matmul(psB[:, 0:CH], w13s_[:, buf, kj, j * 128:(j + 1) * 128],
                              xnb[:, kj, :], start=(kj == 0), stop=(kj == 7))
                for kj in range(8):
                    i = Tq.matmul(psB[:, CH:2 * CH],
                                  w13s_[:, buf, kj, (j + 4) * 128:(j + 5) * 128],
                                  xnb[:, kj, :], start=(kj == 0), stop=(kj == 7))
                i.then_inc(te.h, 1)
                te.inc()
                S.wait_ge(te.h, te.n)
                S.activation(sgt[:, :], psB[:, 0:CH], AF.Silu).then_inc(se.h, 1)
                se.inc()
                V.wait_ge(se.h, se.n)
                if e < 16:
                    V.tensor_mul(sgt[:, :], sgt[:, :], psB[:, CH:2 * CH])
                    V.tensor_mul(sgb[:, j, :], sgt[:, :], bcomb[:, :]).then_inc(ve.h, 1)
                else:
                    V.tensor_mul(sgb[:, j, :], sgt[:, :], psB[:, CH:2 * CH]
                                 ).then_inc(ve.h, 1)
                ve.inc()
            for o in range(8):
                bank = 2 + o % 2
                pdst = psB[:, bank * CH:(bank + 1) * CH]
                Tq.wait_ge(ve.h, ve.n)
                for kj in range(4):
                    i = Tq.matmul(pdst, w2s_[:, buf, kj, o * 128:(o + 1) * 128],
                                  sgb[:, kj, :], start=(kj == 0), stop=(kj == 3))
                i.then_inc(te.h, 1)
                te.inc()
                V.wait_ge(te.h, te.n)
                V.tensor_add(y0acc[:, o, :], y0acc[:, o, :], pdst).then_inc(ve.h, 1)
                ve.inc()
        S.wait_ge(ve.h, ve.n)
        S.activation(y0acc[:, :, :], y0acc[:, :, :], AF.Silu).then_inc(se.h, 1)
        se.inc()
        V.wait_ge(se.h, se.n)
        V.tensor_mul(y0acc[:, :, :], y0acc[:, :, :], s0buf[:, :, :])
        V.tensor_add(x1c[:, :, :], xrbuf[:, :, :], y0acc[:, :, :]).then_inc(ve.h, 1)
        ve.inc()
        Y.wait_ge(ve.h, ve.n)
        Y.dma_start(out=io["x1T"][:, tok].rearrange("(a p) t -> p a t", p=128),
                    in_=x1c[:, :, :]).then_inc(dy.h, 16)
        dy.inc(16)
    Y.wait_ge(dy.h, dy.n)


def build_phase_d(dbg=False):
    nc = bass.Bass("TRN2", target_bir_lowering=False)
    io = {}
    if dbg:
        for nm, shape in (("dbg_sct", [16, 4, CH]), ("dbg_stm", [128, 4, 16]),
                          ("dbg_btm2", [128, 4, 16]), ("dbg_selm", [128, 4, 16]),
                          ("dbg_combT", [16, CH])):
            io[nm] = nc.dram_tensor(nm, shape, F32, kind="ExternalOutput").ap()
    io["xn0Tb"] = nc.dram_tensor("xn0Tb", [D, T], BF16, kind="ExternalInput").ap()
    io["xn0T"] = nc.dram_tensor("xn0T", [D, T], F32, kind="ExternalInput").ap()
    io["xT"] = nc.dram_tensor("xT", [D, T], F32, kind="ExternalInput").ap()
    io["s0T"] = nc.dram_tensor("s0T", [D, T], F32, kind="ExternalInput").ap()
    io["gate0T"] = nc.dram_tensor("gate0T", [D, 16], F32, kind="ExternalInput").ap()
    io["gate0b"] = nc.dram_tensor("gate0b", [16, 1], F32, kind="ExternalInput").ap()
    io["sel16"] = nc.dram_tensor("sel16", [16, 16 * 128], F32, kind="ExternalInput").ap()
    io["identf"] = nc.dram_tensor("identf", [128, 128], F32, kind="ExternalInput").ap()
    io["w13b"] = nc.dram_tensor("w13b", [16, D, D], BF16, kind="ExternalInput").ap()
    io["w20b"] = nc.dram_tensor("w20b", [16, 512, D], BF16, kind="ExternalInput").ap()
    io["w13sb"] = nc.dram_tensor("w13sb", [D, D], BF16, kind="ExternalInput").ap()
    io["w2sb"] = nc.dram_tensor("w2sb", [512, D], BF16, kind="ExternalInput").ap()
    io["x1T"] = nc.dram_tensor("x1T", [D, T], F32, kind="ExternalOutput").ap()
    with contextlib.ExitStack() as stack:
        ctx = Ctx(nc, stack)
        emit_phase_d(ctx, io)
        ctx.finish()
    return nc


def emit_phase_f(ctx, io):
    Tq, V, S, G, Y = ctx.T, ctx.V, ctx.S, ctx.G, ctx.Y
    te, ve, se = ctx.sem("te"), ctx.sem("ve"), ctx.sem("se")
    dl, dy, gy = ctx.sem("dl"), ctx.sem("dy"), ctx.sem("gy")

    wq = ctx.sb("wq", [128, 8, D], BF16)
    wk = ctx.sb("wk", [128, 8, 64], BF16)
    wv = ctx.sb("wv", [128, 8, 64], BF16)
    bq8 = ctx.sb("bq8_s", [128, 8], F32)
    bkc = ctx.sb("bkc", [64, 1], F32)
    bvc = ctx.sb("bvc", [64, 1], F32)
    identb = ctx.sb("identb_s", [128, 128], BF16)
    xbuf = ctx.sb("xbuf", [128, 8, CH], F32)
    xnl = ctx.sb("xnl", [128, 8, CH], F32)
    xnlb = ctx.sb("xnlb_f", [128, 8, CH], BF16)
    kTe = ctx.sb("kTe", [65, T], BF16)
    vtme = ctx.sb("vtme", [128, 32, 65], BF16)
    qch = ctx.sb("qch", [65, 16, CH], BF16)
    q2 = ctx.sb("q2", [64, CH], F32)
    kn2 = ctx.sb("kn2", [1, CH], F32)
    kmax2 = ctx.sb("kmax2", [1, 2], F32)
    qn2 = ctx.sb("qn2", [1, 16, CH], F32)
    mrow = ctx.sb("mrow", [1, 16, CH], F32)
    ones64 = ctx.sb("ones64", [64, 1], F32)
    psA = ctx.ps("psA", [128, 2048], F32)
    psB = ctx.ps("psB", [128, 2048], F32)

    G.dma_start(out=wq[:, :, :],
                in_=io["Wq"].rearrange("(a p) o -> p a o", p=128)).then_inc(gy.h, 16)
    G.dma_start(out=wk[:, :, :],
                in_=io["Wk"].rearrange("(a p) o -> p a o", p=128)).then_inc(gy.h, 16)
    G.dma_start(out=wv[:, :, :],
                in_=io["Wv"].rearrange("(a p) o -> p a o", p=128)).then_inc(gy.h, 16)
    G.dma_start(out=identb[:, :], in_=io["identf"]).then_inc(gy.h, 16)
    gy.inc(64)
    Y.dma_start(out=bq8[:, :], in_=io["bq8"]).then_inc(dl.h, 16)
    Y.dma_start(out=bkc[:, :], in_=io["bk"]).then_inc(dl.h, 16)
    Y.dma_start(out=bvc[:, :], in_=io["bv"]).then_inc(dl.h, 16)
    dl.inc(48)
    init_common(ctx)
    V.wait_ge(dl.h, dl.n)
    V.wait_ge(gy.h, gy.n)
    V.memset(kTe[64:65, :], 1.0)
    V.memset(vtme[:, :, 64:65], 1.0)
    V.tensor_scalar_mul(bq8[:, :], bq8[:, :], 0.125)
    V.memset(ones64[:, :], 1.0)
    V.memset(kmax2[:, :], 0.0).then_inc(ve.h, 1)
    ve.inc()
    onesc = ctx.tiles["onesc"]

    for c in range(NCH):
        tok = slice(c * CH, (c + 1) * CH)
        Y.wait_ge(ve.h, ve.n)
        Y.wait_ge(se.h, se.n)
        Y.dma_start(out=xbuf[:, :, :],
                    in_=io["x1T"][:, tok].rearrange("(a p) t -> p a t", p=128)
                    ).then_inc(dl.h, 16)
        dl.inc(16)
        for e in (V, S, Tq):
            e.wait_ge(dl.h, dl.n)
        emit_layernorm_chunk(ctx, xbuf[:, :, :], 8, out_f32=xnl[:, :, :],
                             out_bf16=xnlb[:, :, :], name="lnl")
        Y.wait_ge(ve.h, ve.n)
        Y.dma_start(out=io["xnlT"][:, tok].rearrange("(a p) t -> p a t", p=128),
                    in_=xnl[:, :, :]).then_inc(dy.h, 16)
        Y.dma_start(out=io["xnlTb"][:, tok].rearrange("(a p) t -> p a t", p=128),
                    in_=xnlb[:, :, :]).then_inc(dy.h, 16)
        dy.inc(32)
        Tq.wait_ge(ve.h, ve.n)
        for kj in range(8):
            Tq.matmul(psB[0:64, 0:CH], wk[:, kj, :], xnlb[:, kj, :],
                      start=(kj == 0), stop=(kj == 7))
        for kj in range(8):
            i = Tq.matmul(psB[0:64, CH:2 * CH], wv[:, kj, :], xnlb[:, kj, :],
                          start=(kj == 0), stop=(kj == 7))
        i.then_inc(te.h, 1)
        te.inc()
        S.wait_ge(te.h, te.n)
        S.activation(kTe[0:64, tok], psB[0:64, 0:CH], AF.Identity, bias=bkc[:, 0:1])
        S.activation(qch[0:64, 0, :], psB[0:64, CH:2 * CH], AF.Identity,
                     bias=bvc[:, 0:1]).then_inc(se.h, 1)
        se.inc()
        S.activation(q2[:, :], psB[0:64, 0:CH], AF.Square).then_inc(se.h, 1)
        se.inc()
        V.wait_ge(se.h, se.n)
        V.tensor_copy(q2[:, :], kTe[0:64, tok]).then_inc(ve.h, 1)
        ve.inc()
        S.wait_ge(ve.h, ve.n)
        S.activation(q2[:, :], q2[:, :], AF.Square).then_inc(se.h, 1)
        se.inc()
        Tq.wait_ge(se.h, se.n)
        Tq.matmul(psB[0:1, 1024:1024 + CH], ones64[:, :], q2[:, :],
                  start=True, stop=True).then_inc(te.h, 1)
        te.inc()
        V.wait_ge(te.h, te.n)
        ctx.vedge(V.tensor_reduce(kmax2[0:1, 0:1], psB[0:1, 1024:1024 + CH],
                                  AX.X, ALU.max))
        ctx.vedge(V.tensor_max(kmax2[0:1, 1:2], kmax2[0:1, 1:2], kmax2[0:1, 0:1]))
        Tq.wait_ge(se.h, se.n)
        for q in range(4):
            i = Tq.transpose(psB[:, 1536 + q * 32:1536 + (q + 1) * 32].bitcast(BF16),
                             qch[0:64, 0, q * 128:(q + 1) * 128], identb[0:64, 0:64])
        i.then_inc(te.h, 1)
        te.inc()
        V.wait_ge(te.h, te.n)
        for q in range(4):
            i = V.tensor_copy(vtme[:, c * 4 + q, 0:64],
                              psB[:, 1536 + q * 32:1536 + (q + 1) * 32].bitcast(BF16))
        i.then_inc(ve.h, 1)
        ve.inc()
    Y.wait_ge(ve.h, ve.n)
    Y.wait_ge(se.h, se.n)
    Y.dma_start(out=io["kTeT"], in_=kTe[:, :]).then_inc(dy.h, 16)
    Y.dma_start(out=io["vtmeT"], in_=vtme[:, :, :]).then_inc(dy.h, 16)
    dy.inc(32)

    for c in range(NCH):
        tok = slice(c * CH, (c + 1) * CH)
        Y.wait_ge(ve.h, ve.n)
        Y.wait_ge(se.h, se.n)
        Y.dma_start(out=xnlb[:, :, :],
                    in_=io["xnlTb"][:, tok].rearrange("(a p) t -> p a t", p=128)
                    ).then_inc(dl.h, 16)
        dl.inc(16)
        Tq.wait_ge(dl.h, dl.n)
        Tq.wait_ge(se.h, se.n)
        for o in range(8):
            bank = o % 2
            pdst = psB[:, bank * CH:(bank + 1) * CH]
            Tq.wait_ge(se.h, se.n)
            for kj in range(8):
                i = Tq.matmul(pdst, wq[:, kj, o * 128:(o + 1) * 128], xnlb[:, kj, :],
                              start=(kj == 0), stop=(kj == 7))
            i.then_inc(te.h, 1)
            te.inc()
            S.wait_ge(te.h, te.n)
            S.activation(qch[0:64, 2 * o, :], pdst[0:64, :], AF.Identity,
                         scale=0.125, bias=bq8[0:64, o:o + 1])
            S.activation(qch[0:64, 2 * o + 1, :], pdst[64:128, :], AF.Identity,
                         scale=0.125, bias=bq8[64:128, o:o + 1]).then_inc(se.h, 1)
            se.inc()
        for h in range(16):
            S.wait_ge(se.h, se.n)
            S.activation(q2[:, :], qch[0:64, h, :], AF.Square).then_inc(se.h, 1)
            se.inc()
            Tq.wait_ge(se.h, se.n)
            Tq.matmul(psB[0:1, 1024 + (h % 2) * CH:1024 + (h % 2) * CH + CH],
                      ones64[:, :], q2[:, :], start=True, stop=True).then_inc(te.h, 1)
            te.inc()
            V.wait_ge(te.h, te.n)
            i = V.tensor_copy(qn2[0:1, h, :],
                              psB[0:1, 1024 + (h % 2) * CH:1024 + (h % 2) * CH + CH])
            i.then_inc(ve.h, 1)
            ve.inc()
        V.tensor_scalar(mrow[:, :, :], qn2[:, :, :], kmax2[0:1, 1:2], None,
                        ALU.mult).then_inc(ve.h, 1)
        ve.inc()
        S.wait_ge(ve.h, ve.n)
        S.activation(mrow[:, :, :], mrow[:, :, :], AF.Ln,
                     bias=ctx.tiles["eps1"][0:1, 0:1])
        S.activation(mrow[:, :, :], mrow[:, :, :], AF.Exp, scale=0.5).then_inc(se.h, 1)
        se.inc()
        V.wait_ge(se.h, se.n)
        V.tensor_scalar_mul(qch[64:65, :, :], mrow[:, :, :], -1.0).then_inc(ve.h, 1)
        ve.inc()
        Y.wait_ge(ve.h, ve.n)
        Y.dma_start(out=io["qTeT"][:, :, tok], in_=qch[:, :, :]).then_inc(dy.h, 16)
        dy.inc(16)
    Y.wait_ge(dy.h, dy.n)


def build_phase_f():
    nc = bass.Bass("TRN2", target_bir_lowering=False)
    io = {}
    io["x1T"] = nc.dram_tensor("x1T", [D, T], F32, kind="ExternalInput").ap()
    io["Wq"] = nc.dram_tensor("Wq", [D, D], F32, kind="ExternalInput").ap()
    io["bq8"] = nc.dram_tensor("bq8", [128, 8], F32, kind="ExternalInput").ap()
    io["Wk"] = nc.dram_tensor("Wk", [D, 64], F32, kind="ExternalInput").ap()
    io["bk"] = nc.dram_tensor("bk", [64, 1], F32, kind="ExternalInput").ap()
    io["Wv"] = nc.dram_tensor("Wv", [D, 64], F32, kind="ExternalInput").ap()
    io["bv"] = nc.dram_tensor("bv", [64, 1], F32, kind="ExternalInput").ap()
    io["identf"] = nc.dram_tensor("identf", [128, 128], F32, kind="ExternalInput").ap()
    io["xnlT"] = nc.dram_tensor("xnlT", [D, T], F32, kind="ExternalOutput").ap()
    io["xnlTb"] = nc.dram_tensor("xnlTb", [D, T], BF16, kind="ExternalOutput").ap()
    io["qTeT"] = nc.dram_tensor("qTeT", [65, 16, T], BF16, kind="ExternalOutput").ap()
    io["kTeT"] = nc.dram_tensor("kTeT", [65, T], BF16, kind="ExternalOutput").ap()
    io["vtmeT"] = nc.dram_tensor("vtmeT", [128, 32, 65], BF16, kind="ExternalOutput").ap()
    with contextlib.ExitStack() as stack:
        ctx = Ctx(nc, stack)
        emit_phase_f(ctx, io)
        ctx.finish()
    return nc


def emit_phase_g(ctx, io):
    Tq, V, S, G, Y = ctx.T, ctx.V, ctx.S, ctx.G, ctx.Y
    te, ve, se = ctx.sem("te"), ctx.sem("ve"), ctx.sem("se")
    dl, dy, gy = ctx.sem("dl"), ctx.sem("dy"), ctx.sem("gy")

    kTe = ctx.sb("kTe", [65, T], BF16)
    vtme = ctx.sb("vtme", [128, 32, 65], BF16)
    qch = ctx.sb("qch", [65, 16, CH], BF16)
    pT = ctx.sb("pT", [128, 32, CH], BF16)
    rrow = ctx.sb("rrow", [1, CH], F32)
    rb = ctx.sb("rb", [128, CH], F32)
    atto = ctx.sb("atto", [128, 8, CH], BF16)
    wo = ctx.sb("wo", [128, 8, D], BF16)
    bo8 = ctx.sb("bo8_s", [128, 8], F32)
    s1c = ctx.sb("s1c", [128, 8, CH], F32)
    psA = ctx.ps("psA", [128, 2048], F32)
    psB = ctx.ps("psB", [128, 2048], F32)

    G.dma_start(out=wo[:, :, :],
                in_=io["Wo"].rearrange("(a p) o -> p a o", p=128)).then_inc(gy.h, 16)
    gy.inc(16)
    Y.dma_start(out=kTe[:, :], in_=io["kTeT"]).then_inc(dl.h, 16)
    Y.dma_start(out=vtme[:, :, :], in_=io["vtmeT"]).then_inc(dl.h, 16)
    Y.dma_start(out=bo8[:, :], in_=io["bo8"]).then_inc(dl.h, 16)
    dl.inc(48)
    init_common(ctx)
    V.wait_ge(dl.h, dl.n)
    ones1 = ctx.tiles["ones1"]
    Tq.wait_ge(gy.h, gy.n)
    Tq.wait_ge(dl.h, dl.n)

    for c in range(NCH):
        tok = slice(c * CH, (c + 1) * CH)
        Y.wait_ge(ve.h, ve.n)
        Y.wait_ge(se.h, se.n)
        Y.dma_start(out=qch[:, :, :], in_=io["qTeT"][:, :, tok]).then_inc(dl.h, 16)
        dl.inc(16)
        for h in range(16):
            Tq.wait_ge(dl.h, dl.n)
            Tq.wait_ge(ve.h, ve.n)
            for kt in range(32):
                bank = kt % 2
                Tq.wait_ge(se.h, se.n - 1 if kt >= 2 else se.n)
                i = Tq.matmul(psB[:, bank * CH:(bank + 1) * CH],
                              kTe[:, kt * 128:(kt + 1) * 128], qch[:, h, :],
                              start=True, stop=True)
                i.then_inc(te.h, 1)
                te.inc()
                S.wait_ge(te.h, te.n)
                S.activation(pT[:, kt, :], psB[:, bank * CH:(bank + 1) * CH],
                             AF.Exp).then_inc(se.h, 1)
                se.inc()
            Tq.wait_ge(se.h, se.n)
            for kt in range(32):
                i = Tq.matmul(psA[0:65, 0:CH], vtme[:, kt, :], pT[:, kt, :],
                              start=(kt == 0), stop=(kt == 31))
            i.then_inc(te.h, 1)
            te.inc()
            V.wait_ge(te.h, te.n)
            ctx.vedge(V.reciprocal(rrow[0:1, :], psA[64:65, 0:CH]))
            V.sem_inc(ve.h, 1)
            ve.inc()
            Tq.wait_ge(ve.h, ve.n)
            Tq.matmul(psA[:, CH:2 * CH], ones1[:, :], rrow[0:1, :],
                      start=True, stop=True).then_inc(te.h, 1)
            te.inc()
            S.wait_ge(te.h, te.n)
            S.activation(rb[:, :], psA[:, CH:2 * CH], AF.Identity).then_inc(se.h, 1)
            se.inc()
            V.wait_ge(se.h, se.n)
            V.tensor_mul(atto[(h % 2) * 64:(h % 2) * 64 + 64, h // 2, :],
                         psA[0:64, 0:CH], rb[0:64, :]).then_inc(ve.h, 1)
            ve.inc()
            if c == 0 and h == 0 and "dbg_pT" in io:
                V.tensor_copy(ctx.sb("dbg_pv_s", [65, CH], F32)[:, :],
                              psA[0:65, 0:CH]).then_inc(ve.h, 1)
                ve.inc()
                Y.wait_ge(ve.h, ve.n)
                Y.dma_start(out=io["dbg_pT"], in_=pT[:, :, :]).then_inc(dy.h, 16)
                Y.dma_start(out=io["dbg_rrow"], in_=rrow[:, :]).then_inc(dy.h, 16)
                Y.dma_start(out=io["dbg_rb"], in_=rb[:, :]).then_inc(dy.h, 16)
                Y.dma_start(out=io["dbg_pv"],
                            in_=ctx.tiles["dbg_pv_s"][:, :]).then_inc(dy.h, 16)
                Y.dma_start(out=io["dbg_atto"], in_=atto[:, :, :]).then_inc(dy.h, 16)
                dy.inc(80)
                for e in (Tq, S):
                    e.wait_ge(dy.h, dy.n)
        for o in range(8):
            bank = o % 2
            pdst = psB[:, bank * CH:(bank + 1) * CH]
            Tq.wait_ge(ve.h, ve.n)
            Tq.wait_ge(se.h, se.n)
            for kj in range(8):
                i = Tq.matmul(pdst, wo[:, kj, o * 128:(o + 1) * 128], atto[:, kj, :],
                              start=(kj == 0), stop=(kj == 7))
            i.then_inc(te.h, 1)
            te.inc()
            S.wait_ge(te.h, te.n)
            S.activation(s1c[:, o, :], pdst, AF.Identity,
                         bias=bo8[:, o:o + 1]).then_inc(se.h, 1)
            se.inc()
        Y.wait_ge(se.h, se.n)
        Y.dma_start(out=io["s1T"][:, tok].rearrange("(a p) t -> p a t", p=128),
                    in_=s1c[:, :, :]).then_inc(dy.h, 16)
        dy.inc(16)
    Y.wait_ge(dy.h, dy.n)


def build_phase_g(dbg=False):
    nc = bass.Bass("TRN2", target_bir_lowering=False)
    io = {}
    if dbg:
        io["dbg_pT"] = nc.dram_tensor("dbg_pT", [128, 32, CH], BF16,
                                      kind="ExternalOutput").ap()
        io["dbg_rrow"] = nc.dram_tensor("dbg_rrow", [1, CH], F32,
                                        kind="ExternalOutput").ap()
        io["dbg_rb"] = nc.dram_tensor("dbg_rb", [128, CH], F32,
                                      kind="ExternalOutput").ap()
        io["dbg_pv"] = nc.dram_tensor("dbg_pv", [65, CH], F32,
                                      kind="ExternalOutput").ap()
        io["dbg_atto"] = nc.dram_tensor("dbg_atto", [128, 8, CH], BF16,
                                        kind="ExternalOutput").ap()
    io["qTeT"] = nc.dram_tensor("qTeT", [65, 16, T], BF16, kind="ExternalInput").ap()
    io["kTeT"] = nc.dram_tensor("kTeT", [65, T], BF16, kind="ExternalInput").ap()
    io["vtmeT"] = nc.dram_tensor("vtmeT", [128, 32, 65], BF16, kind="ExternalInput").ap()
    io["Wo"] = nc.dram_tensor("Wo", [D, D], F32, kind="ExternalInput").ap()
    io["bo8"] = nc.dram_tensor("bo8", [128, 8], F32, kind="ExternalInput").ap()
    io["s1T"] = nc.dram_tensor("s1T", [D, T], F32, kind="ExternalOutput").ap()
    with contextlib.ExitStack() as stack:
        ctx = Ctx(nc, stack)
        emit_phase_g(ctx, io)
        ctx.finish()
    return nc


def emit_phase_h(ctx, io):
    Tq, V, S, G, Y = ctx.T, ctx.V, ctx.S, ctx.G, ctx.Y
    te, ve, se = ctx.sem("te"), ctx.sem("ve"), ctx.sem("se")
    dl, dy, gy = ctx.sem("dl"), ctx.sem("dy"), ctx.sem("gy")

    g1w = ctx.sb("g1w", [128, 8, 8], F32)
    sel8 = ctx.sb("sel8_s", [8, 8 * 128], F32)
    identf = ctx.sb("identf_s", [128, 128], F32)
    xnlb = ctx.sb("xnlb_f", [128, 8, CH], BF16)
    xnlf = ctx.sb("xnlf", [128, 8, CH], F32)
    x1c = ctx.sb("x1c", [128, 8, CH], F32)
    s1buf = ctx.sb("s1buf", [128, 8, CH], F32)
    y1acc = ctx.sb("y1acc", [128, 8, CH], F32)
    outc = ctx.sb("outc", [128, 8, CH], F32)
    zeb = ctx.sb("zeb", [128, 8, CH], BF16)
    gT = ctx.sb("gT", [128, 22, CH], BF16)
    sgt = ctx.sb("sgt", [128, CH], F32)
    pb = ctx.sb("pb", [128, CH], F32)
    w13sl = ctx.sb("w13sl", [128, 2, 8, 256], BF16)
    w2sl = ctx.sb("w2sl", [128, 2, 22, 128], BF16)
    lgr = ctx.sb("lgr", [8, CH], F32)
    lg_tm = ctx.sb("lg_tm", [128, 4, 8], F32)
    prb = ctx.sb("prb", [128, 4, 8], F32)
    selm = ctx.sb("selm8", [128, 4, 8], F32)
    mskh = ctx.sb("mskh", [128, 4, 8], F32)
    rcol = ctx.sb("rcol8", [128, 4, 1], F32)
    p8T = ctx.sb("p8T", [8, CH], F32)
    psA = ctx.ps("psA", [128, 2048], F32)
    psB = ctx.ps("psB", [128, 2048], F32)

    Y.dma_start(out=g1w[:, :, :],
                in_=io["gate1T"].rearrange("(a p) e -> p a e", p=128)).then_inc(dl.h, 16)
    Y.dma_start(out=sel8[:, :], in_=io["sel8"]).then_inc(dl.h, 16)
    Y.dma_start(out=identf[:, :], in_=io["identf"]).then_inc(dl.h, 16)
    dl.inc(48)
    init_common(ctx)
    V.wait_ge(dl.h, dl.n)
    id8 = identf[0:8, 0:8]

    def load_w13(e, o, buf):
        Y.dma_start(out=w13sl[:, buf, :, :],
                    in_=io["w13xb"][e, :, o * 256:(o + 1) * 256]
                    .rearrange("(a p) o -> p a o", p=128)).then_inc(dl.h, 16)
        return dl.inc(16)

    def load_w2(e, o2, buf):
        Y.dma_start(out=w2sl[:, buf, :, :],
                    in_=io["w21b"][e, :, o2 * 128:(o2 + 1) * 128]
                    .rearrange("(a p) o -> p a o", p=128)).then_inc(dl.h, 16)
        return dl.inc(16)

    for c in range(NCH):
        tok = slice(c * CH, (c + 1) * CH)
        Y.wait_ge(ve.h, ve.n)
        Y.wait_ge(se.h, se.n)
        Y.dma_start(out=xnlb[:, :, :],
                    in_=io["xnlTb"][:, tok].rearrange("(a p) t -> p a t", p=128)
                    ).then_inc(dl.h, 16)
        Y.dma_start(out=xnlf[:, :, :],
                    in_=io["xnlT"][:, tok].rearrange("(a p) t -> p a t", p=128)
                    ).then_inc(dl.h, 16)
        Y.dma_start(out=x1c[:, :, :],
                    in_=io["x1T"][:, tok].rearrange("(a p) t -> p a t", p=128)
                    ).then_inc(dl.h, 16)
        Y.dma_start(out=s1buf[:, :, :],
                    in_=io["s1T"][:, tok].rearrange("(a p) t -> p a t", p=128)
                    ).then_inc(dl.h, 16)
        dl.inc(64)
        Tq.wait_ge(dl.h, dl.n)
        Tq.wait_ge(ve.h, ve.n)
        for kj in range(8):
            i = Tq.matmul(psA[0:8, 0:CH], g1w[:, kj, :], xnlf[:, kj, :],
                          start=(kj == 0), stop=(kj == 7))
        i.then_inc(te.h, 1)
        te.inc()
        V.wait_ge(te.h, te.n)
        V.tensor_copy(lgr[:, :], psA[0:8, 0:CH]).then_inc(ve.h, 1)
        ve.inc()
        Tq.wait_ge(ve.h, ve.n)
        for q in range(4):
            i = Tq.transpose(psB[0:128, q * 8:(q + 1) * 8],
                             lgr[:, q * 128:(q + 1) * 128], id8)
        i.then_inc(te.h, 1)
        te.inc()
        V.wait_ge(te.h, te.n)
        V.tensor_copy(lg_tm[:, :, :], psB[:, 0:32].rearrange("p (q e) -> p q e", e=8)
                      ).then_inc(ve.h, 1)
        ve.inc()
        for q in range(4):
            ctx.vedge(V.tensor_reduce(rcol[:, q, :], lg_tm[:, q, :], AX.X, ALU.max))
            V.tensor_scalar(lg_tm[:, q, :], lg_tm[:, q, :], rcol[:, q, 0:1], None,
                            ALU.subtract)
        V.sem_inc(ve.h, 1)
        ve.inc()
        S.wait_ge(ve.h, ve.n)
        S.activation(prb[:, :, :], lg_tm[:, :, :], AF.Exp).then_inc(se.h, 1)
        se.inc()
        V.wait_ge(se.h, se.n)
        for q in range(4):
            ctx.vedge(V.tensor_reduce(rcol[:, q, :], prb[:, q, :], AX.X, ALU.add))
            ctx.vedge(V.reciprocal(rcol[:, q, :], rcol[:, q, :]))
            V.tensor_scalar(prb[:, q, :], prb[:, q, :], rcol[:, q, 0:1], None,
                            ALU.mult)
        V.sem_inc(ve.h, 1)
        ve.inc()
        V.tensor_copy(mskh[:, :, :], prb[:, :, :])
        V.memset(selm[:, :, :], 0.0)
        for r in range(2):
            for q in range(4):
                ctx.vedge(V.tensor_reduce(rcol[:, q, :], mskh[:, q, :], AX.X, ALU.max))
                V.tensor_scalar(mskh[:, q, :], mskh[:, q, :], rcol[:, q, 0:1], None,
                                ALU.is_ge)
            V.tensor_add(selm[:, :, :], selm[:, :, :], mskh[:, :, :])
            if r < 1:
                V.scalar_tensor_tensor(mskh[:, :, :], selm[:, :, :], -2.0,
                                       prb[:, :, :], ALU.mult, ALU.add)
        V.tensor_scalar_min(selm[:, :, :], selm[:, :, :], 1.0)
        V.tensor_mul(selm[:, :, :], selm[:, :, :], prb[:, :, :]).then_inc(ve.h, 1)
        ve.inc()
        Tq.wait_ge(ve.h, ve.n)
        for q in range(4):
            i = Tq.transpose(psB[0:8, 256 + q * 128:256 + (q + 1) * 128],
                             selm[:, q, :], identf[:, :])
        i.then_inc(te.h, 1)
        te.inc()
        V.wait_ge(te.h, te.n)
        for q in range(4):
            i = V.tensor_copy(p8T[:, q * 128:(q + 1) * 128],
                              psB[0:8, 256 + q * 128:256 + (q + 1) * 128])
        i.then_inc(ve.h, 1)
        ve.inc()
        V.memset(y1acc[:, :, :], 0.0).then_inc(ve.h, 1)
        ve.inc()
        lw13 = {}
        lw2 = {}
        lw13[(0, 0)] = load_w13(0, 0, 0)
        for e in range(8):
            Tq.wait_ge(ve.h, ve.n)
            Tq.matmul(psA[:, 512:512 + CH], sel8[:, e * 128:(e + 1) * 128],
                      p8T[:, :], start=True, stop=True).then_inc(te.h, 1)
            te.inc()
            S.wait_ge(te.h, te.n)
            S.activation(pb[:, :], psA[:, 512:512 + CH], AF.Identity).then_inc(se.h, 1)
            se.inc()
            V.wait_ge(se.h, se.n)
            V.tensor_tensor(zeb[:, :, :], xnlb[:, :, :],
                            pb[:, :].unsqueeze(1).to_broadcast([128, 8, CH]),
                            ALU.mult).then_inc(ve.h, 1)
            ve.inc()
            for o in range(22):
                buf = o % 2
                nxt = (e, o + 1) if o + 1 < 22 else ((e + 1, 0) if e + 1 < 8 else None)
                if nxt is not None and nxt not in lw13:
                    Y.wait_ge(te.h, te.n)
                    lw13[nxt] = load_w13(nxt[0], nxt[1], 1 - buf)
                Tq.wait_ge(dl.h, lw13[(e, o)])
                Tq.wait_ge(ve.h, ve.n)
                Tq.wait_ge(se.h, se.n)
                for kj in range(8):
                    Tq.matmul(psB[:, 0:CH], w13sl[:, buf, kj, 0:128],
                              zeb[:, kj, :], start=(kj == 0), stop=(kj == 7))
                for kj in range(8):
                    i = Tq.matmul(psB[:, CH:2 * CH], w13sl[:, buf, kj, 128:256],
                                  zeb[:, kj, :], start=(kj == 0), stop=(kj == 7))
                i.then_inc(te.h, 1)
                te.inc()
                S.wait_ge(te.h, te.n)
                S.activation(sgt[:, :], psB[:, 0:CH], AF.Silu).then_inc(se.h, 1)
                se.inc()
                V.wait_ge(se.h, se.n)
                V.tensor_mul(gT[:, o, :], sgt[:, :], psB[:, CH:2 * CH]).then_inc(ve.h, 1)
                ve.inc()
            Y.wait_ge(te.h, te.n)
            lw2[(e, 0)] = load_w2(e, 0, 0)
            for o2 in range(8):
                buf = o2 % 2
                nxt = (e, o2 + 1) if o2 + 1 < 8 else None
                if nxt is not None:
                    Y.wait_ge(te.h, te.n)
                    lw2[nxt] = load_w2(e, o2 + 1, 1 - buf)
                Tq.wait_ge(dl.h, lw2[(e, o2)])
                Tq.wait_ge(ve.h, ve.n)
                bank = 2 + o2 % 2
                pdst = psB[:, bank * CH:(bank + 1) * CH]
                for kj in range(22):
                    i = Tq.matmul(pdst, w2sl[:, buf, kj, :], gT[:, kj, :],
                                  start=(kj == 0), stop=(kj == 21))
                i.then_inc(te.h, 1)
                te.inc()
                V.wait_ge(te.h, te.n)
                V.tensor_add(y1acc[:, o2, :], y1acc[:, o2, :], pdst).then_inc(ve.h, 1)
                ve.inc()
        S.wait_ge(ve.h, ve.n)
        S.activation(y1acc[:, :, :], y1acc[:, :, :], AF.Silu).then_inc(se.h, 1)
        se.inc()
        V.wait_ge(se.h, se.n)
        V.tensor_mul(y1acc[:, :, :], y1acc[:, :, :], s1buf[:, :, :])
        V.tensor_add(outc[:, :, :], x1c[:, :, :], y1acc[:, :, :]).then_inc(ve.h, 1)
        ve.inc()
        Y.wait_ge(ve.h, ve.n)
        Y.dma_start(out=io["outT"][:, tok].rearrange("(a p) t -> p a t", p=128),
                    in_=outc[:, :, :]).then_inc(dy.h, 16)
        dy.inc(16)
    Y.wait_ge(dy.h, dy.n)


def build_phase_h():
    nc = bass.Bass("TRN2", target_bir_lowering=False)
    io = {}
    io["xnlTb"] = nc.dram_tensor("xnlTb", [D, T], BF16, kind="ExternalInput").ap()
    io["xnlT"] = nc.dram_tensor("xnlT", [D, T], F32, kind="ExternalInput").ap()
    io["x1T"] = nc.dram_tensor("x1T", [D, T], F32, kind="ExternalInput").ap()
    io["s1T"] = nc.dram_tensor("s1T", [D, T], F32, kind="ExternalInput").ap()
    io["gate1T"] = nc.dram_tensor("gate1T", [D, 8], F32, kind="ExternalInput").ap()
    io["sel8"] = nc.dram_tensor("sel8", [8, 8 * 128], F32, kind="ExternalInput").ap()
    io["identf"] = nc.dram_tensor("identf", [128, 128], F32, kind="ExternalInput").ap()
    io["w13xb"] = nc.dram_tensor("w13xb", [8, D, 5632], BF16, kind="ExternalInput").ap()
    io["w21b"] = nc.dram_tensor("w21b", [8, 2816, D], BF16, kind="ExternalInput").ap()
    io["outT"] = nc.dram_tensor("outT", [D, T], F32, kind="ExternalOutput").ap()
    with contextlib.ExitStack() as stack:
        ctx = Ctx(nc, stack)
        emit_phase_h(ctx, io)
        ctx.finish()
    return nc


def build_phase_w():
    nc = bass.Bass("TRN2", target_bir_lowering=False)
    specs = [
        ("w13T", [16, D, D], "w13b"),
        ("w20T", [16, 512, D], "w20b"),
        ("w13sT", [1, D, D], "w13sb2"),
        ("w2sT", [1, 512, D], "w2sb2"),
        ("w13x", [8, D, 5632], "w13xb"),
        ("w21T", [8, 2816, D], "w21b"),
    ]
    with contextlib.ExitStack() as stack:
        sem = stack.enter_context(nc.semaphore("cast"))
        block = stack.enter_context(nc.Block())
        n = 0
        calls = []
        for nm_in, shape, nm_out in specs:
            src = nc.dram_tensor(nm_in, shape, F32, kind="ExternalInput").ap()
            dst = nc.dram_tensor(nm_out, shape, BF16, kind="ExternalOutput").ap()
            for e in range(shape[0]):
                calls.append((dst[e], src[e]))
                n += 16
        def body(g):
            for dstap, srcap in calls:
                g.dma_start(out=dstap, in_=srcap).then_inc(sem, 16)
            g.wait_ge(sem, n)
        block.gpsimd(body)
    return nc


def emit_phase_w(ctx, pairs):
    G = ctx.G
    gy = ctx.sem("gy")
    for src, dst, n in pairs:
        for e in range(n):
            G.dma_start(out=dst[e] if n > 1 else dst,
                        in_=src[e] if n > 1 else src).then_inc(gy.h, 16)
            gy.inc(16)
    G.wait_ge(gy.h, gy.n)

def build_fused():
    nc = bass.Bass("TRN2", target_bir_lowering=False)
    I, O, N = "ExternalInput", "ExternalOutput", "Internal"
    t = {}
    def dram(nm, shape, dt_, kind):
        t[nm] = nc.dram_tensor(nm, list(shape), dt_, kind=kind).ap()
        return t[nm]
    dram("xT", [D, T], F32, I)
    for nm, shape in (("inprojT", [D, DPROJ]), ("lnw8", [128, 8]), ("lnb8", [128, 8]),
                      ("convw", [CONVD, 4]), ("convb", [128, 18]), ("dtbias", [32, 1]),
                      ("alog", [32, 1]), ("drep", [128, 16]), ("normw16", [128, 16]),
                      ("outprojT", [DI, D]), ("gate0T", [D, 16]), ("gate0b", [16, 1]),
                      ("w13T", [16, D, D]), ("w20T", [16, 512, D]),
                      ("w13sT", [1, D, D]), ("w2sT", [1, 512, D]),
                      ("Wq", [D, D]), ("bq8", [128, 8]), ("Wk", [D, 64]),
                      ("bk", [64, 1]), ("Wv", [D, 64]), ("bv", [64, 1]),
                      ("Wo", [D, D]), ("bo8", [128, 8]), ("gate1T", [D, 8]),
                      ("w13x", [8, D, 5632]), ("w21T", [8, 2816, D]),
                      ("identf", [128, 128]), ("maskneg", [128, 128]),
                      ("rowsel32", [32, NH * 128]), ("sel16", [16, 16 * 128]),
                      ("sel8", [8, 8 * 128])):
        dram(nm, shape, F32, I)
    for nm, shape, dt_ in (("xn0T", [D, T], F32), ("xn0Tb", [D, T], BF16),
                           ("szT", [DI, T], F32), ("convoT", [CONVD, T], BF16),
                           ("dtT", [32, T], F32), ("yT", [DI, T], F32),
                           ("s0T", [D, T], F32), ("x1T", [D, T], F32),
                           ("xnlT", [D, T], F32), ("xnlTb", [D, T], BF16),
                           ("qTeT", [65, 16, T], BF16), ("kTeT", [65, T], BF16),
                           ("vtmeT", [128, 32, 65], BF16), ("s1T", [D, T], F32),
                           ("w13b", [16, D, D], BF16), ("w20b", [16, 512, D], BF16),
                           ("w13sb2", [1, D, D], BF16), ("w2sb2", [1, 512, D], BF16),
                           ("w13xb", [8, D, 5632], BF16), ("w21b", [8, 2816, D], BF16)):
        dram(nm, shape, dt_, N)
    dram("outT", [D, T], F32, O)

    io = dict(t)
    io["w13sb"] = t["w13sb2"][0]
    io["w2sb"] = t["w2sb2"][0]

    with contextlib.ExitStack() as outer:
        ctx = Ctx(nc, outer)
        init_common(ctx)
        ctx.ps("psA", [128, 2048], F32)
        ctx.ps("psB", [128, 2048], F32)
        block = outer.enter_context(nc.Block())
        phases = [
            lambda: emit_phase_w(ctx, [
                (t["w13T"], t["w13b"], 16), (t["w20T"], t["w20b"], 16),
                (t["w13sT"], t["w13sb2"], 1), (t["w2sT"], t["w2sb2"], 1),
                (t["w13x"], t["w13xb"], 8), (t["w21T"], t["w21b"], 8)]),
            lambda: emit_phase_a(ctx, io),
            lambda: emit_phase_b(ctx, io),
            lambda: emit_phase_c(ctx, io),
            lambda: emit_phase_d(ctx, io),
            lambda: emit_phase_f(ctx, io),
            lambda: emit_phase_g(ctx, io),
            lambda: emit_phase_h(ctx, io),
        ]
        for pi, phase in enumerate(phases):
            ctx.nphase = pi
            with contextlib.ExitStack() as scope:
                ctx.scope = scope
                ctx.scope_tiles = {}
                if pi > 0:
                    barrier(ctx)
                phase()
                ctx.replay_into(block)
            ctx.scope = None
    return nc

def prep_fused_inputs(xT_core, w, consts):
    m = {"xT": xT_core}
    for k in ("inprojT", "lnw8", "lnb8", "convw", "convb", "dtbias", "alog", "drep",
              "normw16", "outprojT", "gate0T", "gate0b", "w13T", "w20T", "w13sT",
              "w2sT", "Wq", "bq8", "Wk", "bk", "Wv", "bv", "Wo", "bo8", "gate1T",
              "w13x", "w21T"):
        m[k] = w[k]
    for k in ("identf", "maskneg", "rowsel32", "sel16", "sel8"):
        m[k] = consts[k]
    return m


def _col8(v):
    return np.ascontiguousarray(v.reshape(8, 128).T)

def prep_phase_a_inputs(xT_core, w):
    return {
        "xT": xT_core,
        "inprojT": w["inprojT"],
        "lnw8": w["lnw8"], "lnb8": w["lnb8"],
        "convw": w["convw"], "convb": w["convb"],
        "dtbias": w["dtbias"],
    }

def build_phase_a():
    nc = bass.Bass("TRN2", target_bir_lowering=False)
    io = {}
    io["xT"] = nc.dram_tensor("xT", [D, T], F32, kind="ExternalInput").ap()
    io["inprojT"] = nc.dram_tensor("inprojT", [D, DPROJ], F32, kind="ExternalInput").ap()
    io["lnw8"] = nc.dram_tensor("lnw8", [128, 8], F32, kind="ExternalInput").ap()
    io["lnb8"] = nc.dram_tensor("lnb8", [128, 8], F32, kind="ExternalInput").ap()
    io["convw"] = nc.dram_tensor("convw", [CONVD, 4], F32, kind="ExternalInput").ap()
    io["convb"] = nc.dram_tensor("convb", [128, 18], F32, kind="ExternalInput").ap()
    io["dtbias"] = nc.dram_tensor("dtbias", [32, 1], F32, kind="ExternalInput").ap()
    io["xn0T"] = nc.dram_tensor("xn0T", [D, T], F32, kind="ExternalOutput").ap()
    io["xn0Tb"] = nc.dram_tensor("xn0Tb", [D, T], BF16, kind="ExternalOutput").ap()
    io["szT"] = nc.dram_tensor("szT", [DI, T], F32, kind="ExternalOutput").ap()
    io["convoT"] = nc.dram_tensor("convoT", [CONVD, T], BF16, kind="ExternalOutput").ap()
    io["dtT"] = nc.dram_tensor("dtT", [32, T], F32, kind="ExternalOutput").ap()
    with contextlib.ExitStack() as stack:
        ctx = Ctx(nc, stack)
        emit_phase_a(ctx, io)
        ctx.finish()
    return nc

def prep_weights(inp):
    C = np.ascontiguousarray
    w = {}
    w["inprojT"] = C(inp["L0_attn_in_proj_w"].T)
    w["lnw8"] = _col8(inp["L0_attn_ln_w"])
    w["lnb8"] = _col8(inp["L0_attn_ln_b"])
    w["convw"] = C(inp["L0_attn_conv_w"])
    w["convb"] = C(inp["L0_attn_conv_b"].reshape(18, 128).T)
    w["dtbias"] = C(inp["L0_attn_dt_bias"].reshape(32, 1))
    w["alog"] = C(inp["L0_attn_A_log"].reshape(32, 1))
    w["drep"] = C(np.repeat(inp["L0_attn_D"], 64).reshape(16, 128).T)
    w["normw16"] = C(inp["L0_attn_norm_w"].reshape(16, 128).T)
    w["outprojT"] = C(inp["L0_attn_out_proj_w"].T)
    w["gate0T"] = C(inp["L0_ffn_gate_w"].T)
    w["gate0b"] = C(inp["L0_ffn_gate_b"].reshape(16, 1))
    w["w13T"] = C(inp["L0_ffn_w13"].transpose(0, 2, 1))
    w["w20T"] = C(inp["L0_ffn_w2"].transpose(0, 2, 1))
    w["w13sT"] = C(inp["L0_ffn_w13_s"].T.reshape(1, D, D))
    w["w2sT"] = C(inp["L0_ffn_w2_s"].T.reshape(1, 512, D))
    w["Wq"] = C(inp["L1_attn_Wq"])
    w["bq8"] = _col8(inp["L1_attn_bq"])
    w["Wk"] = C(inp["L1_attn_Wk"])
    w["bk"] = C(inp["L1_attn_bk"].reshape(64, 1))
    w["Wv"] = C(inp["L1_attn_Wv"])
    w["bv"] = C(inp["L1_attn_bv"].reshape(64, 1))
    w["Wo"] = C(inp["L1_attn_Wo"])
    w["bo8"] = _col8(inp["L1_attn_bo"])
    w["gate1T"] = C(inp["L1_ffn_gate_w"].T)
    w1 = inp["L1_ffn_w1"].reshape(8, D, 22, 128)
    w3 = inp["L1_ffn_w3"].reshape(8, D, 22, 128)
    w["w13x"] = C(np.concatenate([w1, w3], axis=3).reshape(8, D, 5632))
    w["w21T"] = C(inp["L1_ffn_w2"])
    return w

CORES = list(range(8))

def _run(nc, in_maps):
    return run_bass_kernel_spmd(nc, in_maps, core_ids=CORES).results

def kernel(**inputs):
    inputs = {k: np.asarray(v) for k, v in inputs.items()}
    _ = inputs["L0_ffn_input_ids"]
    x = inputs["L0_attn_x"]
    w = prep_weights(inputs)
    consts = prep_consts()
    xTs = [np.ascontiguousarray(x[b].T) for b in range(8)]
    nc = build_fused()
    res = run_bass_kernel_spmd(
        nc, [prep_fused_inputs(xTs[b], w, consts) for b in CORES], core_ids=CORES)
    return np.ascontiguousarray(
        np.stack([r["outT"].T for r in res.results]).astype(np.float32))

def kernel_unfused(**inputs):
    inputs = {k: np.asarray(v) for k, v in inputs.items()}
    x = inputs["L0_attn_x"]
    w = prep_weights(inputs)
    consts = prep_consts()
    xTs = [np.ascontiguousarray(x[b].T) for b in range(8)]

    rW = _run(build_phase_w(), [{
        "w13T": w["w13T"], "w20T": w["w20T"], "w13sT": w["w13sT"], "w2sT": w["w2sT"],
        "w13x": w["w13x"], "w21T": w["w21T"]} for _ in CORES])
    rA = _run(build_phase_a(), [prep_phase_a_inputs(xTs[b], w) for b in CORES])
    rB = _run(build_phase_b(), [{
        "convoT": rA[b]["convoT"], "dtT": rA[b]["dtT"], "alog": w["alog"],
        "drep": w["drep"], "identf": consts["identf"], "maskneg": consts["maskneg"],
        "rowsel32": consts["rowsel32"]} for b in CORES])
    rC = _run(build_phase_c(), [{
        "yT": rB[b]["yT"], "szT": rA[b]["szT"], "xn0T": rA[b]["xn0T"],
        "outprojT": w["outprojT"], "normw16": w["normw16"]} for b in CORES])
    rD = _run(build_phase_d(), [{
        "xn0Tb": rA[b]["xn0Tb"], "xn0T": rA[b]["xn0T"], "xT": xTs[b],
        "s0T": rC[b]["s0T"], "gate0T": w["gate0T"], "gate0b": w["gate0b"],
        "sel16": consts["sel16"], "identf": consts["identf"],
        "w13b": rW[b]["w13b"], "w20b": rW[b]["w20b"],
        "w13sb": rW[b]["w13sb2"].reshape(D, D),
        "w2sb": rW[b]["w2sb2"].reshape(512, D)} for b in CORES])
    rF = _run(build_phase_f(), [{
        "x1T": rD[b]["x1T"], "Wq": w["Wq"], "bq8": w["bq8"], "Wk": w["Wk"],
        "bk": w["bk"], "Wv": w["Wv"], "bv": w["bv"],
        "identf": consts["identf"]} for b in CORES])
    rG = _run(build_phase_g(), [{
        "qTeT": rF[b]["qTeT"], "kTeT": rF[b]["kTeT"], "vtmeT": rF[b]["vtmeT"],
        "Wo": w["Wo"], "bo8": w["bo8"]} for b in CORES])
    rH = _run(build_phase_h(), [{
        "xnlTb": rF[b]["xnlTb"], "xnlT": rF[b]["xnlT"], "x1T": rD[b]["x1T"],
        "s1T": rG[b]["s1T"], "gate1T": w["gate1T"], "sel8": consts["sel8"],
        "identf": consts["identf"], "w13xb": rW[b]["w13xb"],
        "w21b": rW[b]["w21b"]} for b in CORES])
    return np.ascontiguousarray(
        np.stack([rH[b]["outT"].T for b in CORES]).astype(np.float32))


# version 03
# speedup vs baseline: 1.3287x; 1.3287x over previous
import contextlib
import numpy as np
import concourse.bass as bass
import concourse.mybir as mybir
from concourse.bass_utils import run_bass_kernel_spmd

F32 = mybir.dt.float32
BF16 = mybir.dt.bfloat16
AF = mybir.ActivationFunctionType
ALU = mybir.AluOpType
AX = mybir.AxisListType

T, D = 4096, 1024
NCH, CH = 8, 512
DI = 2048
CONVD = 2304
DPROJ = 4384
NH, HD, DS = 32, 64, 128
NQC, QC = 32, 128
EPS = 1e-5
NEG = -3.0e38


class Sem:
    def __init__(self, h):
        self.h = h
        self.n = 0
    def inc(self, k=1):
        self.n += k
        return self.n

class _RecInst:
    __slots__ = ("e",)
    def __init__(self, e):
        self.e = e
    def then_inc(self, sem, n):
        self.e[3].append((sem, n))
        return self

class Rec:
    def __init__(self):
        self.calls = []
    def __getattr__(self, name):
        calls = self.calls
        def call(*a, **k):
            e = (name, a, k, [])
            calls.append(e)
            return _RecInst(e)
        return call

def _replay(rec, eng):
    for name, a, k, incs in rec.calls:
        inst = getattr(eng, name)(*a, **k)
        for (sem, n) in incs:
            inst.then_inc(sem, n)

class Ctx:
    def __init__(self, nc, stack):
        self.nc = nc
        self.stack = stack
        self.scope = None
        self.scope_tiles = {}
        self.T = Rec()
        self.V = Rec()
        self.S = Rec()
        self.G = Rec()
        self.Y = Rec()
        self.sems = {}
        self.tiles = {}
        self.psums = {}
        self.nphase = 0
    def sem(self, name):
        if name not in self.sems:
            self.sems[name] = Sem(self.stack.enter_context(self.nc.semaphore(name)))
        return self.sems[name]
    def sbg(self, name, shape, dtype):
        if name not in self.tiles:
            self.tiles[name] = self.stack.enter_context(
                self.nc.sbuf_tensor(name, list(shape), dtype))
        return self.tiles[name]
    def sb(self, name, shape, dtype):
        if self.scope is not None and name in self.scope_tiles:
            return self.scope_tiles[name]
        if name in self.tiles:
            return self.tiles[name]
        if self.scope is None:
            return self.sbg(name, shape, dtype)
        self.scope_tiles[name] = self.scope.enter_context(
            self.nc.sbuf_tensor(f"p{self.nphase}_{name}", list(shape), dtype))
        return self.scope_tiles[name]
    def ps(self, name, shape, dtype=F32):
        if name not in self.psums:
            self.psums[name] = self.stack.enter_context(
                self.nc.psum_tensor(name, list(shape), dtype))
        return self.psums[name]
    def replay_into(self, block):
        block.tensor(lambda e: _replay(self.T, e))
        block.vector(lambda e: _replay(self.V, e))
        block.scalar(lambda e: _replay(self.S, e))
        block.gpsimd(lambda e: _replay(self.G, e))
        block.sync(lambda e: _replay(self.Y, e))
        self.T = Rec(); self.V = Rec(); self.S = Rec()
        self.G = Rec(); self.Y = Rec()
    def vedge(self, inst):
        vx = self.sem("vx")
        inst.then_inc(vx.h, 1)
        vx.inc()
        self.V.wait_ge(vx.h, vx.n)

    def finish(self):
        block = self.stack.enter_context(self.nc.Block())
        block.tensor(lambda e: _replay(self.T, e))
        block.vector(lambda e: _replay(self.V, e))
        block.scalar(lambda e: _replay(self.S, e))
        block.gpsimd(lambda e: _replay(self.G, e))
        block.sync(lambda e: _replay(self.Y, e))

class Banks:
    def __init__(self):
        self.tok = {}
    def wait(self, ctx, bank):
        if bank in self.tok:
            sem, n = self.tok[bank]
            ctx.T.wait_ge(sem.h, n)
    def set(self, bank, sem):
        self.tok[bank] = (sem, sem.n)

ENGS = ("T", "V", "S", "G", "Y")

def barrier(ctx):
    for e in ENGS:
        s = ctx.sem("bar" + e)
        getattr(ctx, e).sem_inc(s.h, 1)
        s.inc()
    for e in ENGS:
        for o in ENGS:
            if o != e:
                so = ctx.sem("bar" + o)
                getattr(ctx, e).wait_ge(so.h, so.n)


def init_common(ctx):
    if "ones1" in ctx.tiles:
        return ctx.tiles["ones1"], ctx.tiles["onesc"]
    V = ctx.V
    ve = ctx.sem("ve")
    ones1 = ctx.sbg("ones1", [1, 128], F32)
    onesc = ctx.sbg("onesc", [128, 1], F32)
    eps1 = ctx.sbg("eps1", [1, 1], F32)
    one32 = ctx.sbg("one32", [32, 1], F32)
    V.memset(ones1[:, :], 1.0)
    V.memset(onesc[:, :], 1.0)
    V.memset(eps1[:, :], EPS)
    V.memset(one32[:, :], 1.0).then_inc(ve.h, 1)
    ve.inc()
    return ones1, onesc

def emit_layernorm_chunk(ctx, src, n_tiles, out_f32=None, out_bf16=None,
                         scale_col=None, bias_col=None, name="ln"):
    nd = n_tiles * 128
    Tq, V, S = ctx.T, ctx.V, ctx.S
    te, ve, se = ctx.sem("te"), ctx.sem("ve"), ctx.sem("se")
    onesc = ctx.tiles["onesc"]
    sq = ctx.sb("ln_sq", [128, 8, CH], F32)[:, 0:n_tiles, :]
    rows = ctx.sb("ln_rows", [1, 4, CH], F32)
    psS = ctx.ps("psA", [128, 2048], F32)

    S.wait_ge(ve.h, ve.n)
    S.activation(sq[:, :, :], src, AF.Square).then_inc(se.h, 1)
    se.inc()
    Tq.wait_ge(se.h, se.n)
    for j in range(n_tiles):
        Tq.matmul(psS[0:1, 0:CH], onesc[:, :], src[:, j, :],
                  start=(j == 0), stop=(j == n_tiles - 1))
    for j in range(n_tiles):
        i = Tq.matmul(psS[0:1, CH:2 * CH], onesc[:, :], sq[:, j, :],
                      start=(j == 0), stop=(j == n_tiles - 1))
    i.then_inc(te.h, 1)
    te.inc()
    V.wait_ge(te.h, te.n)
    V.tensor_scalar_mul(rows[0:1, 0, :], psS[0:1, 0:CH], 1.0 / nd)
    V.tensor_mul(rows[0:1, 2, :], rows[0:1, 0, :], rows[0:1, 0, :])
    V.tensor_scalar(rows[0:1, 1, :], psS[0:1, CH:2 * CH], 1.0 / nd, None,
                    ALU.mult)
    V.tensor_sub(rows[0:1, 1, :], rows[0:1, 1, :], rows[0:1, 2, :]).then_inc(ve.h, 1)
    ve.inc()
    S.wait_ge(ve.h, ve.n)
    S.activation(rows[0:1, 2, :], rows[0:1, 1, :], AF.Ln, bias=ctx.tiles["eps1"][0:1, 0:1])
    S.activation(rows[0:1, 1, :], rows[0:1, 2, :], AF.Exp, scale=-0.5).then_inc(se.h, 1)
    se.inc()
    ones1 = ctx.tiles["ones1"]
    Tq.wait_ge(se.h, se.n)
    Tq.matmul(psS[:, 1024:1024 + CH], ones1[:, :], rows[0:1, 0, :], start=True, stop=True)
    Tq.matmul(psS[:, 1536:1536 + CH], ones1[:, :], rows[0:1, 1, :],
              start=True, stop=True).then_inc(te.h, 1)
    te.inc()
    V.wait_ge(te.h, te.n)
    mb = psS[:, 1024:1024 + CH].unsqueeze(1).to_broadcast([128, n_tiles, CH])
    rb = psS[:, 1536:1536 + CH].unsqueeze(1).to_broadcast([128, n_tiles, CH])
    V.tensor_tensor(sq[:, :, :], src, mb, ALU.subtract)
    if scale_col is None and out_f32 is not None:
        V.tensor_tensor(out_f32, sq[:, :, :], rb, ALU.mult).then_inc(ve.h, 1)
        ve.inc()
        if out_bf16 is not None:
            V.tensor_copy(out_bf16, out_f32).then_inc(ve.h, 1)
            ve.inc()
    else:
        V.tensor_tensor(sq[:, :, :], sq[:, :, :], rb, ALU.mult).then_inc(ve.h, 1)
        ve.inc()
        S.wait_ge(ve.h, ve.n)
        for j in range(n_tiles):
            i = S.activation(out_bf16[:, j, :] if out_f32 is None else out_f32[:, j, :],
                             sq[:, j, :], AF.Identity,
                             bias=bias_col[:, j:j + 1], scale=scale_col[:, j:j + 1])
        i.then_inc(se.h, 1)
        se.inc()


def emit_phase_a(ctx, io):
    nc = ctx.nc
    Tq, V, S, G, Y = ctx.T, ctx.V, ctx.S, ctx.G, ctx.Y
    te, ve, se = ctx.sem("te"), ctx.sem("ve"), ctx.sem("se")
    dl, dy, gy = ctx.sem("dl"), ctx.sem("dy"), ctx.sem("gy")

    wip = ctx.sb("wip", [128, 8, DPROJ], BF16)
    lnw = ctx.sb("lnw", [128, 8], F32)
    lnb = ctx.sb("lnb", [128, 8], F32)
    cw = ctx.sb("cw", [128, 18, 4], F32)
    cb = ctx.sb("cb", [128, 18], F32)
    dtb = ctx.sb("dtb", [32, 1], F32)
    xbuf = ctx.sb("xbuf", [128, 8, CH], F32)
    xn0 = ctx.sb("xn0", [128, 8, CH], F32)
    xn0b = ctx.sb("xn0b", [128, 8, CH], BF16)
    xn1b = ctx.sb("xn1b", [128, 8, CH], BF16)
    szt = ctx.sb("szt", [128, 2, CH], F32)
    xbc = ctx.sb("xbc", [128, 18, 3 + CH], F32)
    cvo = ctx.sb("cvo", [128, 2, CH], BF16)
    cvt = ctx.sb("cvt", [128, CH], F32)
    dtw = ctx.sb("dtw", [32, 3, CH], F32)
    psB = ctx.ps("psB", [128, 2048], F32)

    G.dma_start(out=wip[:, :, :],
                in_=io["inprojT"].rearrange("(a p) o -> p a o", p=128)).then_inc(gy.h, 16)
    gy.inc(16)
    Y.dma_start(out=lnw[:, :], in_=io["lnw8"]).then_inc(dl.h, 16)
    Y.dma_start(out=lnb[:, :], in_=io["lnb8"]).then_inc(dl.h, 16)
    Y.dma_start(out=cw[:, :, :],
                in_=io["convw"].rearrange("(a p) j -> p a j", p=128)).then_inc(dl.h, 16)
    Y.dma_start(out=cb[:, :], in_=io["convb"]).then_inc(dl.h, 16)
    Y.dma_start(out=dtb[:, :], in_=io["dtbias"]).then_inc(dl.h, 16)
    dl.inc(16 * 5)
    V.wait_ge(dl.h, dl.n)
    V.memset(xbc[:, :, 0:3], 0.0).then_inc(ve.h, 1)
    ve.inc()
    init_common(ctx)
    Tq.wait_ge(gy.h, gy.n)

    for c in range(NCH):
        tok = slice(c * CH, (c + 1) * CH)
        Y.wait_ge(ve.h, ve.n)
        Y.wait_ge(se.h, se.n)
        Y.dma_start(out=xbuf[:, :, :],
                    in_=io["xT"][:, tok].rearrange("(a p) t -> p a t", p=128)
                    ).then_inc(dl.h, 16)
        dl.inc(16)
        V.wait_ge(dy.h, dy.n)
        S.wait_ge(dy.h, dy.n)
        for e in (V, S, Tq):
            e.wait_ge(dl.h, dl.n)
        emit_layernorm_chunk(ctx, xbuf[:, :, :], 8, out_f32=xn0[:, :, :],
                             out_bf16=xn0b[:, :, :], name="ln0")
        Y.wait_ge(ve.h, ve.n)
        Y.dma_start(out=io["xn0T"][:, tok].rearrange("(a p) t -> p a t", p=128),
                    in_=xn0[:, :, :]).then_inc(dy.h, 16)
        Y.dma_start(out=io["xn0Tb"][:, tok].rearrange("(a p) t -> p a t", p=128),
                    in_=xn0b[:, :, :]).then_inc(dy.h, 16)
        dy.inc(32)
        emit_layernorm_chunk(ctx, xn0[:, :, :], 8, out_f32=None, out_bf16=xn1b[:, :, :],
                             scale_col=lnw, bias_col=lnb, name="ln1")

        zdma = [0, 0]
        bk = Banks() if c == 0 else ctx.banksA
        ctx.banksA = bk
        Tq.wait_ge(se.h, se.n)
        for o in range(35):
            m = 128 if o < 34 else 32
            bank = o % 4
            pdst = psB[0:m, bank * CH:(bank + 1) * CH]
            bk.wait(ctx, bank)
            for kj in range(8):
                i = Tq.matmul(pdst, wip[:, kj, o * 128:o * 128 + m],
                              xn1b[:, kj, :], start=(kj == 0), stop=(kj == 7))
            i.then_inc(te.h, 1)
            te.inc()
            if o < 16:
                b2 = o % 2
                S.wait_ge(te.h, te.n)
                S.wait_ge(dy.h, zdma[b2])
                S.activation(szt[:, b2, :], pdst, AF.Silu).then_inc(se.h, 1)
                se.inc()
                bk.set(bank, se)
                Y.wait_ge(se.h, se.n)
                Y.dma_start(out=io["szT"][o * 128:(o + 1) * 128, tok],
                            in_=szt[:, b2, :]).then_inc(dy.h, 16)
                zdma[b2] = dy.inc(16)
            elif o < 34:
                V.wait_ge(te.h, te.n)
                V.tensor_copy(xbc[:, o - 16, 3:3 + CH], pdst).then_inc(ve.h, 1)
                ve.inc()
                bk.set(bank, ve)
            else:
                V.wait_ge(te.h, te.n)
                V.tensor_copy(dtw[:, 0, :], pdst).then_inc(ve.h, 1)
                ve.inc()
                bk.set(bank, ve)
                S.wait_ge(ve.h, ve.n)
                S.activation(dtw[:, 1, :], dtw[:, 0, :], AF.Abs, bias=dtb[:, 0:1])
                S.activation(dtw[:, 1, :], dtw[:, 1, :], AF.Exp, scale=-1.0)
                S.activation(dtw[:, 1, :], dtw[:, 1, :], AF.Ln,
                             bias=ctx.tiles["one32"][:, 0:1])
                S.activation(dtw[:, 2, :], dtw[:, 0, :], AF.Relu,
                             bias=dtb[:, 0:1]).then_inc(se.h, 1)
                se.inc()
                V.wait_ge(se.h, se.n)
                V.tensor_add(dtw[:, 0, :], dtw[:, 1, :], dtw[:, 2, :]).then_inc(ve.h, 1)
                ve.inc()
                Y.wait_ge(ve.h, ve.n)
                Y.dma_start(out=io["dtT"][:, tok], in_=dtw[:, 0, :]).then_inc(dy.h, 16)
                dy.inc(16)
        cdma = [0, 0]
        for a in range(18):
            b2 = a % 2
            V.wait_ge(ve.h, ve.n)
            V.tensor_scalar_mul(cvt[:, :], xbc[:, a, 0:CH], cw[:, a, 0:1])
            for j in range(1, 4):
                i = V.scalar_tensor_tensor(cvt[:, :], xbc[:, a, j:j + CH],
                                           cw[:, a, j:j + 1], cvt[:, :],
                                           ALU.mult, ALU.add)
            i.then_inc(ve.h, 1)
            ve.inc()
            S.wait_ge(ve.h, ve.n)
            S.wait_ge(dy.h, cdma[b2])
            S.activation(cvo[:, b2, :], cvt[:, :], AF.Silu,
                         bias=cb[:, a:a + 1]).then_inc(se.h, 1)
            se.inc()
            Y.wait_ge(se.h, se.n)
            Y.dma_start(out=io["convoT"][a * 128:(a + 1) * 128, tok],
                        in_=cvo[:, b2, :]).then_inc(dy.h, 16)
            cdma[b2] = dy.inc(16)
            V.wait_ge(se.h, se.n)
            V.tensor_copy(xbc[:, a, 0:3], xbc[:, a, CH:CH + 3]).then_inc(ve.h, 1)
            ve.inc()
    Y.wait_ge(dy.h, dy.n)



def emit_phase_b(ctx, io):
    Tq, V, S, G, Y = ctx.T, ctx.V, ctx.S, ctx.G, ctx.Y
    te, ve, se = ctx.sem("te"), ctx.sem("ve"), ctx.sem("se")
    dl, dy, gy = ctx.sem("dl"), ctx.sem("dy"), ctx.sem("gy")

    dtf = ctx.sb("dtf", [32, T], F32)
    alog = ctx.sb("alog_s", [32, 1], F32)
    acol = ctx.sb("acol", [32, 1], F32)
    drep = ctx.sb("drep_s", [128, 16], F32)
    identf = ctx.sb("identf_s", [128, 128], F32)
    identb = ctx.sb("identb", [128, 128], BF16)
    maskneg = ctx.sb("maskneg_s", [128, 128], F32)
    rsel = ctx.sb("rsel", [32, NH * 128], F32)
    ones32 = ctx.sb("ones32", [32, 128], F32)
    zer32 = ctx.sb("zer32", [32, QC], F32)
    xcb = ctx.sb("xcb", [128, 16, QC], BF16)
    Bc = ctx.sb("Bc", [128, QC], BF16)
    Cc = ctx.sb("Cc", [128, QC], BF16)
    acm = ctx.sb("acm", [32, QC], F32)
    cum = ctx.sb("cum", [32, QC], F32)
    negc = ctx.sb("negc", [32, QC], F32)
    ecum = ctx.sb("ecum", [32, QC], F32)
    diagc = ctx.sb("diagc", [32, 32], F32)
    cumT = ctx.sb("cumT", [128, 32], F32)
    dt_tm = ctx.sb("dt_tm", [128, 32], F32)
    cendb = ctx.sb("cendb", [128, 32], F32)
    decb = ctx.sb("decb", [128, 32], F32)
    wfac = ctx.sb("wfac", [128, 32], F32)
    x_tm = ctx.sb("x_tm", [128, DI], BF16)
    dtx = ctx.sb("dtx", [128, DI], BF16)
    wx = ctx.sb("wx", [128, DI], BF16)
    B_tm = ctx.sb("B_tm", [128, QC], BF16)
    g_sb = ctx.sb("g_sb", [128, QC], F32)
    dm_sb = ctx.sb("dm_sb", [128, 512], F32)
    dm2 = ctx.sb("dm2", [128, 512], F32)
    Stb = ctx.sb("Stb", [128, 512], BF16)
    St2b = ctx.sb("St2b", [128, 512], BF16)
    hT = ctx.sb("hT", [128, DI], F32)
    hTb = ctx.sb("hTb", [128, DI], BF16)
    ych = ctx.sb("ych", [128, 16, QC], F32)
    psA = ctx.ps("psA", [128, 2048], F32)
    psB = ctx.ps("psB", [128, 2048], F32)

    Y.dma_start(out=dtf[:, :], in_=io["dtT"]).then_inc(dl.h, 16)
    Y.dma_start(out=alog[:, :], in_=io["alog"]).then_inc(dl.h, 16)
    Y.dma_start(out=drep[:, :], in_=io["drep"]).then_inc(dl.h, 16)
    Y.dma_start(out=identf[:, :], in_=io["identf"]).then_inc(dl.h, 16)
    Y.dma_start(out=maskneg[:, :], in_=io["maskneg"]).then_inc(dl.h, 16)
    Y.dma_start(out=rsel[:, :], in_=io["rowsel32"]).then_inc(dl.h, 16)
    dl.inc(16 * 6)
    G.dma_start(out=identb[:, :], in_=io["identf"]).then_inc(gy.h, 16)
    gy.inc(16)
    V.wait_ge(dl.h, dl.n)
    V.wait_ge(gy.h, gy.n)
    V.memset(ones32[:, :], 1.0)
    V.memset(zer32[:, :], 0.0)
    V.memset(hT[:, :], 0.0)
    V.memset(hTb[:, :], 0.0).then_inc(ve.h, 1)
    ve.inc()
    S.wait_ge(ve.h, ve.n)
    S.activation(acol[:, :], alog[:, :], AF.Exp).then_inc(se.h, 1)
    se.inc()
    V.wait_ge(se.h, se.n)
    ctx.vedge(V.tensor_sub(acol[:, :], zer32[:, 0:1], acol[:, :]))
    V.sem_inc(ve.h, 1)
    ve.inc()

    id32 = identf[0:32, 0:32]
    for c in range(NQC):
        tok = slice(c * QC, (c + 1) * QC)
        Y.wait_ge(ve.h, ve.n)
        Y.wait_ge(te.h, te.n)
        Y.dma_start(out=xcb[:, :, :],
                    in_=io["convoT"][0:DI, tok].rearrange("(a p) t -> p a t", p=128)
                    ).then_inc(dl.h, 16)
        Y.dma_start(out=Bc[:, :], in_=io["convoT"][DI:DI + 128, tok]).then_inc(dl.h, 16)
        Y.dma_start(out=Cc[:, :], in_=io["convoT"][DI + 128:DI + 256, tok]
                    ).then_inc(dl.h, 16)
        dl.inc(48)
        V.wait_ge(dy.h, dy.n)
        V.wait_ge(dl.h, dl.n)
        V.tensor_scalar_mul(acm[:, :], dtf[:, tok], acol[:, 0:1])
        V.tensor_tensor_scan(cum[:, :], acm[:, :], acm[:, :], 0.0, ALU.add, ALU.bypass)
        ctx.vedge(V.tensor_sub(negc[:, :], zer32[:, :], cum[:, :]))
        V.tensor_mul(diagc[:, :], id32, cum[:, QC - 1:QC].to_broadcast([32, 32])
                     ).then_inc(ve.h, 1)
        ve.inc()
        S.wait_ge(ve.h, ve.n)
        S.activation(ecum[:, :], cum[:, :], AF.Exp).then_inc(se.h, 1)
        se.inc()
        b2 = 1024
        Tq.wait_ge(ve.h, ve.n)
        Tq.wait_ge(dl.h, dl.n)
        Tq.transpose(psB[:, b2 + 0:b2 + 32], cum[:, :], id32)
        Tq.transpose(psB[:, b2 + 32:b2 + 64], dtf[:, tok], id32)
        Tq.matmul(psB[:, b2 + 64:b2 + 96], ones32[:, :], diagc[:, :],
                  start=True, stop=True)
        Tq.matmul(psB[:, b2 + 160:b2 + 288], Bc[:, :], Cc[:, :], start=True, stop=True)
        Tq.transpose(psB[:, b2 + 96:b2 + 160].bitcast(BF16), Bc[:, :],
                     identb[:, :]).then_inc(te.h, 1)
        te.inc()
        V.wait_ge(te.h, te.n)
        V.tensor_copy(cumT[:, :], psB[:, b2 + 0:b2 + 32])
        V.tensor_copy(dt_tm[:, :], psB[:, b2 + 32:b2 + 64])
        V.tensor_copy(cendb[:, :], psB[:, b2 + 64:b2 + 96])
        V.tensor_copy(g_sb[:, :], psB[:, b2 + 160:b2 + 288])
        V.tensor_copy(B_tm[:, :], psB[:, b2 + 96:b2 + 160].bitcast(BF16)
                      ).then_inc(ve.h, 1)
        ve.inc()
        S.wait_ge(ve.h, ve.n)
        S.activation(decb[:, :], cendb[:, :], AF.Exp).then_inc(se.h, 1)
        se.inc()
        V.tensor_sub(wfac[:, :], cendb[:, :], cumT[:, :]).then_inc(ve.h, 1)
        ve.inc()
        S.wait_ge(ve.h, ve.n)
        S.activation(wfac[:, :], wfac[:, :], AF.Exp).then_inc(se.h, 1)
        se.inc()
        V.wait_ge(se.h, se.n)
        ctx.vedge(V.tensor_mul(wfac[:, :], wfac[:, :], dt_tm[:, :]))
        V.sem_inc(ve.h, 1)
        ve.inc()
        b3 = 1536
        for r in range(2):
            Tq.wait_ge(ve.h, ve.n)
            for j in range(8):
                i = Tq.transpose(psB[:, b3 + j * 64:b3 + (j + 1) * 64].bitcast(BF16),
                                 xcb[:, r * 8 + j, :], identb[:, :])
            i.then_inc(te.h, 1)
            te.inc()
            V.wait_ge(te.h, te.n)
            for j in range(8):
                i = V.tensor_copy(x_tm[:, (r * 8 + j) * 128:(r * 8 + j + 1) * 128],
                                  psB[:, b3 + j * 64:b3 + (j + 1) * 64].bitcast(BF16))
            i.then_inc(ve.h, 1)
            ve.inc()
        x3 = x_tm[:, :].rearrange("p (h q) -> p h q", q=HD)
        V.tensor_tensor(dtx[:, :].rearrange("p (h q) -> p h q", q=HD), x3,
                        dt_tm[:, :].unsqueeze(2).to_broadcast([128, NH, HD]), ALU.mult)
        V.tensor_tensor(wx[:, :].rearrange("p (h q) -> p h q", q=HD), x3,
                        wfac[:, :].unsqueeze(2).to_broadcast([128, NH, HD]),
                        ALU.mult).then_inc(ve.h, 1)
        ve.inc()
        for g in range(8):
            heads = range(4 * g, 4 * g + 4)
            Tq.wait_ge(ve.h, ve.n)
            for j, h in enumerate(heads):
                Tq.matmul(psB[:, j * 128:(j + 1) * 128],
                          rsel[:, h * 128:(h + 1) * 128], cum[:, :],
                          start=(j == 0), stop=False)
            Tq.matmul(psB[:, 0:512], negc[:, :], rsel[:, 4 * g * 128:4 * g * 128 + 512],
                      start=False, stop=True).then_inc(te.h, 1)
            te.inc()
            Tq.wait_ge(se.h, se.n)
            for j, h in enumerate(heads):
                i = Tq.matmul(psB[:, 512 + j * 128:512 + (j + 1) * 128],
                              rsel[:, h * 128:(h + 1) * 128], ecum[:, :],
                              start=(j == 0), stop=(j == 3))
            i.then_inc(te.h, 1)
            te.inc()
            V.wait_ge(te.h, te.n)
            V.tensor_tensor(dm_sb[:, :].rearrange("p (r n) -> p r n", r=4),
                            psB[:, 0:512].rearrange("p (r n) -> p r n", r=4),
                            maskneg[:, :].unsqueeze(1).to_broadcast([128, 4, 128]),
                            ALU.add).then_inc(ve.h, 1)
            ve.inc()
            S.wait_ge(ve.h, ve.n)
            S.activation(dm2[:, :], dm_sb[:, :], AF.Exp).then_inc(se.h, 1)
            se.inc()
            V.wait_ge(se.h, se.n)
            V.tensor_tensor(Stb[:, :].rearrange("p (r n) -> p r n", r=4),
                            dm2[:, :].rearrange("p (r n) -> p r n", r=4),
                            g_sb[:, :].unsqueeze(1).to_broadcast([128, 4, 128]),
                            ALU.mult)
            V.tensor_tensor(St2b[:, :].rearrange("p (r n) -> p r n", r=4),
                            psB[:, 512:1024].rearrange("p (r n) -> p r n", r=4),
                            Cc[:, :].unsqueeze(1).to_broadcast([128, 4, 128]),
                            ALU.mult).then_inc(ve.h, 1)
            ve.inc()
            Tq.wait_ge(ve.h, ve.n)
            for j, h in enumerate(heads):
                slot = h % 16
                out = psA[0:64, slot * 128:(slot + 1) * 128]
                Tq.matmul(out, dtx[:, h * HD:(h + 1) * HD], Stb[:, j * 128:(j + 1) * 128],
                          start=(j == 0), stop=False, skip_group_check=True)
                i = Tq.matmul(out, hTb[:, h * HD:(h + 1) * HD],
                              St2b[:, j * 128:(j + 1) * 128],
                              start=False, stop=(j == 3), skip_group_check=True)
            i.then_inc(te.h, 1)
            te.inc()
            V.wait_ge(te.h, te.n)
            for j, h in enumerate(heads):
                slot = h % 16
                pair, half = h // 2, h % 2
                rows = slice(half * 64, (half + 1) * 64)
                i = V.scalar_tensor_tensor(ych[rows, pair, :], xcb[rows, pair, :],
                                           drep[rows, pair:pair + 1],
                                           psA[0:64, slot * 128:(slot + 1) * 128],
                                           ALU.mult, ALU.add)
            i.then_inc(ve.h, 1)
            ve.inc()
        Tq.wait_ge(ve.h, ve.n)
        for q in range(4):
            i = Tq.matmul(psB[:, q * 512:(q + 1) * 512], B_tm[:, :],
                          wx[:, q * 512:(q + 1) * 512], start=True, stop=True,
                          skip_group_check=True)
        i.then_inc(te.h, 1)
        te.inc()
        V.wait_ge(te.h, te.n)
        V.tensor_tensor(hT[:, :].rearrange("p (h q) -> p h q", q=HD),
                        hT[:, :].rearrange("p (h q) -> p h q", q=HD),
                        decb[:, :].unsqueeze(2).to_broadcast([128, NH, HD]), ALU.mult)
        V.tensor_add(hT[:, :], hT[:, :], psB[:, :])
        V.tensor_copy(hTb[:, :], hT[:, :]).then_inc(ve.h, 1)
        ve.inc()
        Y.wait_ge(ve.h, ve.n)
        Y.dma_start(out=io["yT"][:, tok].rearrange("(a p) t -> p a t", p=128),
                    in_=ych[:, :, :]).then_inc(dy.h, 16)
        dy.inc(16)
        if c == 0 and "dbg_cum" in io:
            for nm, tile in (("dbg_cum", cum), ("dbg_g", g_sb), ("dbg_stb", Stb),
                             ("dbg_st2b", St2b), ("dbg_dtx", dtx), ("dbg_hT", hT),
                             ("dbg_xtm", x_tm), ("dbg_wfac", wfac),
                             ("dbg_cumT", cumT), ("dbg_dttm", dt_tm),
                             ("dbg_btm", B_tm)):
                Y.dma_start(out=io[nm], in_=tile[:, :]).then_inc(dy.h, 16)
                dy.inc(16)
    Y.wait_ge(dy.h, dy.n)


def build_phase_b(dbg=False):
    nc = bass.Bass("TRN2", target_bir_lowering=False)
    io = {}
    if dbg:
        for nm, shape, dt_ in (("dbg_cum", [32, QC], F32), ("dbg_g", [128, QC], F32),
                               ("dbg_stb", [128, 512], BF16), ("dbg_st2b", [128, 512], BF16),
                               ("dbg_dtx", [128, DI], BF16), ("dbg_hT", [128, DI], F32),
                               ("dbg_xtm", [128, DI], BF16), ("dbg_wfac", [128, 32], F32),
                               ("dbg_cumT", [128, 32], F32), ("dbg_dttm", [128, 32], F32),
                               ("dbg_btm", [128, QC], BF16)):
            io[nm] = nc.dram_tensor(nm, shape, dt_, kind="ExternalOutput").ap()
    io["convoT"] = nc.dram_tensor("convoT", [CONVD, T], BF16, kind="ExternalInput").ap()
    io["dtT"] = nc.dram_tensor("dtT", [32, T], F32, kind="ExternalInput").ap()
    io["alog"] = nc.dram_tensor("alog", [32, 1], F32, kind="ExternalInput").ap()
    io["drep"] = nc.dram_tensor("drep", [128, 16], F32, kind="ExternalInput").ap()
    io["identf"] = nc.dram_tensor("identf", [128, 128], F32, kind="ExternalInput").ap()
    io["maskneg"] = nc.dram_tensor("maskneg", [128, 128], F32, kind="ExternalInput").ap()
    io["rowsel32"] = nc.dram_tensor("rowsel32", [32, NH * 128], F32,
                                    kind="ExternalInput").ap()
    io["yT"] = nc.dram_tensor("yT", [DI, T], F32, kind="ExternalOutput").ap()
    with contextlib.ExitStack() as stack:
        ctx = Ctx(nc, stack)
        emit_phase_b(ctx, io)
        ctx.finish()
    return nc

def prep_consts():
    s = np.arange(128)
    consts = {}
    consts["identf"] = np.eye(128, dtype=np.float32)
    consts["maskneg"] = np.where(s[:, None] <= s[None, :], 0.0, NEG).astype(np.float32)
    rsel = np.zeros((32, NH * 128), np.float32)
    for h in range(NH):
        rsel[h, h * 128:(h + 1) * 128] = 1.0
    consts["rowsel32"] = rsel
    sel16 = np.zeros((16, 16 * 128), np.float32)
    for e in range(16):
        sel16[e, e * 128:(e + 1) * 128] = 1.0
    consts["sel16"] = sel16
    consts["sel8"] = np.ascontiguousarray(sel16[:8, :8 * 128])
    return consts


def emit_phase_c(ctx, io):
    Tq, V, S, G, Y = ctx.T, ctx.V, ctx.S, ctx.G, ctx.Y
    te, ve, se = ctx.sem("te"), ctx.sem("ve"), ctx.sem("se")
    dl, dy, gy = ctx.sem("dl"), ctx.sem("dy"), ctx.sem("gy")

    wop = ctx.sb("wop", [128, 16, D], BF16)
    nw = ctx.sb("nw", [128, 16], F32)
    ybuf = ctx.sb("ybuf", [128, 16, CH], F32)
    zbuf = ctx.sb("zbuf", [128, 16, CH], F32)
    xnbuf = ctx.sb("xnbuf", [128, 8, CH], F32)
    y3b = ctx.sb("y3b", [128, 16, CH], BF16)
    s0c = ctx.sb("s0c", [128, 8, CH], F32)
    rows = ctx.sb("c_rows", [1, 2, CH], F32)
    psA = ctx.ps("psA", [128, 2048], F32)
    psB = ctx.ps("psB", [128, 2048], F32)

    G.dma_start(out=wop[:, :, :],
                in_=io["outprojT"].rearrange("(a p) o -> p a o", p=128)).then_inc(gy.h, 16)
    gy.inc(16)
    Y.dma_start(out=nw[:, :], in_=io["normw16"]).then_inc(dl.h, 16)
    dl.inc(16)
    init_common(ctx)
    V.wait_ge(dl.h, dl.n)
    Tq.wait_ge(gy.h, gy.n)
    onesc = ctx.tiles["onesc"]
    ones1 = ctx.tiles["ones1"]

    for c in range(NCH):
        tok = slice(c * CH, (c + 1) * CH)
        Y.wait_ge(ve.h, ve.n)
        Y.wait_ge(se.h, se.n)
        Y.dma_start(out=ybuf[:, :, :],
                    in_=io["yT"][:, tok].rearrange("(a p) t -> p a t", p=128)
                    ).then_inc(dl.h, 16)
        Y.dma_start(out=zbuf[:, :, :],
                    in_=io["szT"][:, tok].rearrange("(a p) t -> p a t", p=128)
                    ).then_inc(dl.h, 16)
        Y.dma_start(out=xnbuf[:, :, :],
                    in_=io["xn0T"][:, tok].rearrange("(a p) t -> p a t", p=128)
                    ).then_inc(dl.h, 16)
        dl.inc(48)
        V.wait_ge(dy.h, dy.n)
        V.wait_ge(dl.h, dl.n)
        V.tensor_mul(ybuf[:, :, :], ybuf[:, :, :], zbuf[:, :, :]).then_inc(ve.h, 1)
        ve.inc()
        S.wait_ge(ve.h, ve.n)
        S.activation(zbuf[:, :, :], ybuf[:, :, :], AF.Square).then_inc(se.h, 1)
        se.inc()
        Tq.wait_ge(se.h, se.n)
        for j in range(16):
            i = Tq.matmul(psA[0:1, 0:CH], onesc[:, :], zbuf[:, j, :],
                          start=(j == 0), stop=(j == 15))
        i.then_inc(te.h, 1)
        te.inc()
        V.wait_ge(te.h, te.n)
        V.tensor_scalar_mul(rows[0:1, 0, :], psA[0:1, 0:CH], 1.0 / DI).then_inc(ve.h, 1)
        ve.inc()
        S.wait_ge(ve.h, ve.n)
        S.activation(rows[0:1, 1, :], rows[0:1, 0, :], AF.Ln,
                     bias=ctx.tiles["eps1"][0:1, 0:1])
        S.activation(rows[0:1, 0, :], rows[0:1, 1, :], AF.Exp, scale=-0.5).then_inc(se.h, 1)
        se.inc()
        Tq.wait_ge(se.h, se.n)
        Tq.matmul(psA[:, 1024:1024 + CH], ones1[:, :], rows[0:1, 0, :],
                  start=True, stop=True).then_inc(te.h, 1)
        te.inc()
        V.wait_ge(te.h, te.n)
        V.tensor_tensor(ybuf[:, :, :], ybuf[:, :, :],
                        psA[:, 1024:1024 + CH].unsqueeze(1).to_broadcast([128, 16, CH]),
                        ALU.mult).then_inc(ve.h, 1)
        ve.inc()
        S.wait_ge(ve.h, ve.n)
        for j in range(16):
            i = S.activation(y3b[:, j, :], ybuf[:, j, :], AF.Identity,
                             scale=nw[:, j:j + 1])
        i.then_inc(se.h, 1)
        se.inc()
        Tq.wait_ge(se.h, se.n)
        bkc_ = Banks() if c == 0 else ctx.banksC
        ctx.banksC = bkc_
        for o in range(8):
            bank = o % 4
            pdst = psB[:, bank * CH:(bank + 1) * CH]
            bkc_.wait(ctx, bank)
            for kj in range(16):
                i = Tq.matmul(pdst, wop[:, kj, o * 128:(o + 1) * 128], y3b[:, kj, :],
                              start=(kj == 0), stop=(kj == 15))
            i.then_inc(te.h, 1)
            te.inc()
            V.wait_ge(te.h, te.n)
            V.tensor_add(s0c[:, o, :], xnbuf[:, o, :], pdst).then_inc(ve.h, 1)
            ve.inc()
            bkc_.set(bank, ve)
        Y.wait_ge(ve.h, ve.n)
        Y.dma_start(out=io["s0T"][:, tok].rearrange("(a p) t -> p a t", p=128),
                    in_=s0c[:, :, :]).then_inc(dy.h, 16)
        dy.inc(16)
    Y.wait_ge(dy.h, dy.n)


def build_phase_c():
    nc = bass.Bass("TRN2", target_bir_lowering=False)
    io = {}
    io["yT"] = nc.dram_tensor("yT", [DI, T], F32, kind="ExternalInput").ap()
    io["szT"] = nc.dram_tensor("szT", [DI, T], F32, kind="ExternalInput").ap()
    io["xn0T"] = nc.dram_tensor("xn0T", [D, T], F32, kind="ExternalInput").ap()
    io["outprojT"] = nc.dram_tensor("outprojT", [DI, D], F32, kind="ExternalInput").ap()
    io["normw16"] = nc.dram_tensor("normw16", [128, 16], F32, kind="ExternalInput").ap()
    io["s0T"] = nc.dram_tensor("s0T", [D, T], F32, kind="ExternalOutput").ap()
    with contextlib.ExitStack() as stack:
        ctx = Ctx(nc, stack)
        emit_phase_c(ctx, io)
        ctx.finish()
    return nc


def emit_phase_d(ctx, io):
    Tq, V, S, G, Y = ctx.T, ctx.V, ctx.S, ctx.G, ctx.Y
    te, ve, se = ctx.sem("te"), ctx.sem("ve"), ctx.sem("se")
    dl, dy, gy = ctx.sem("dl"), ctx.sem("dy"), ctx.sem("gy")

    g0w = ctx.sb("g0w", [128, 8, 16], F32)
    g0b = ctx.sb("g0b", [16, 1], F32)
    sel16 = ctx.sb("sel16_s", [16, 16 * 128], F32)
    identf = ctx.sb("identf_s", [128, 128], F32)
    xnb = ctx.sb("xnb", [128, 8, CH], BF16)
    xnf = ctx.sb("xnf", [128, 8, CH], F32)
    xrbuf = ctx.sb("xrbuf", [128, 8, CH], F32)
    s0buf = ctx.sb("s0buf", [128, 8, CH], F32)
    y0acc = ctx.sb("y0acc", [128, 8, CH], F32)
    w13s_ = ctx.sb("w13s_", [128, 2, 8, D], BF16)
    w2s_ = ctx.sb("w2s_", [128, 2, 4, D], BF16)
    sct = ctx.sb("sct", [16, 4, CH], F32)
    s_tm = ctx.sb("s_tm", [128, 4, 16], F32)
    b_tm = ctx.sb("b_tm", [128, 4, 16], F32)
    msk = ctx.sb("msk", [128, 4, 16], F32)
    selm = ctx.sb("selm", [128, 4, 16], F32)
    rcol = ctx.sb("rcol", [128, 4, 1], F32)
    combT = ctx.sb("combT", [16, CH], F32)
    bcomb = ctx.sb("bcomb", [128, CH], F32)
    sgt = ctx.sb("sgt", [128, CH], F32)
    sgb = ctx.sb("sgb", [128, 4, CH], BF16)
    x1c = ctx.sb("x1c", [128, 8, CH], F32)
    psA = ctx.ps("psA", [128, 2048], F32)
    psB = ctx.ps("psB", [128, 2048], F32)

    Y.dma_start(out=g0w[:, :, :],
                in_=io["gate0T"].rearrange("(a p) e -> p a e", p=128)).then_inc(dl.h, 16)
    dl.inc(16)
    Y.dma_start(out=g0b[:, :], in_=io["gate0b"]).then_inc(dl.h, 16)
    Y.dma_start(out=sel16[:, :], in_=io["sel16"]).then_inc(dl.h, 16)
    Y.dma_start(out=identf[:, :], in_=io["identf"]).then_inc(dl.h, 16)
    dl.inc(16 * 3)
    init_common(ctx)
    V.wait_ge(dl.h, dl.n)

    def load_exp(e, buf):
        if e < 16:
            Y.dma_start(out=w13s_[:, buf, :, :],
                        in_=io["w13b"][e].rearrange("(a p) o -> p a o", p=128)
                        ).then_inc(dl.h, 16)
            Y.dma_start(out=w2s_[:, buf, :, :],
                        in_=io["w20b"][e].rearrange("(a p) o -> p a o", p=128)
                        ).then_inc(dl.h, 16)
        else:
            Y.dma_start(out=w13s_[:, buf, :, :],
                        in_=io["w13sb"].rearrange("(a p) o -> p a o", p=128)
                        ).then_inc(dl.h, 16)
            Y.dma_start(out=w2s_[:, buf, :, :],
                        in_=io["w2sb"].rearrange("(a p) o -> p a o", p=128)
                        ).then_inc(dl.h, 16)
        return dl.inc(32)

    id16 = identf[0:16, 0:16]
    for c in range(NCH):
        tok = slice(c * CH, (c + 1) * CH)
        Y.wait_ge(ve.h, ve.n)
        Y.wait_ge(se.h, se.n)
        Y.dma_start(out=xnb[:, :, :],
                    in_=io["xn0Tb"][:, tok].rearrange("(a p) t -> p a t", p=128)
                    ).then_inc(dl.h, 16)
        Y.dma_start(out=xnf[:, :, :],
                    in_=io["xn0T"][:, tok].rearrange("(a p) t -> p a t", p=128)
                    ).then_inc(dl.h, 16)
        Y.dma_start(out=xrbuf[:, :, :],
                    in_=io["xT"][:, tok].rearrange("(a p) t -> p a t", p=128)
                    ).then_inc(dl.h, 16)
        Y.dma_start(out=s0buf[:, :, :],
                    in_=io["s0T"][:, tok].rearrange("(a p) t -> p a t", p=128)
                    ).then_inc(dl.h, 16)
        dl.inc(64)
        ldtoks = {0: load_exp(0, 0)}
        Tq.wait_ge(dl.h, dl.n - 32)
        Tq.wait_ge(ve.h, ve.n)
        for kj in range(8):
            i = Tq.matmul(psA[0:16, 0:CH], g0w[:, kj, :], xnf[:, kj, :],
                          start=(kj == 0), stop=(kj == 7))
        i.then_inc(te.h, 1)
        te.inc()
        S.wait_ge(te.h, te.n)
        S.activation(sct[:, 0, :], psA[0:16, 0:CH], AF.Sigmoid).then_inc(se.h, 1)
        se.inc()
        V.wait_ge(dy.h, dy.n)
        V.wait_ge(se.h, se.n)
        V.tensor_scalar_add(sct[:, 1, :], sct[:, 0, :], g0b[:, 0:1]).then_inc(ve.h, 1)
        ve.inc()
        Tq.wait_ge(ve.h, ve.n)
        for q in range(4):
            Tq.transpose(psB[:, q * 16:(q + 1) * 16], sct[:, 0, q * 128:(q + 1) * 128],
                         id16)
            i = Tq.transpose(psB[:, 64 + q * 16:64 + (q + 1) * 16],
                             sct[:, 1, q * 128:(q + 1) * 128], id16)
        i.then_inc(te.h, 1)
        te.inc()
        V.wait_ge(te.h, te.n)
        V.tensor_copy(s_tm[:, :, :], psB[:, 0:64].rearrange("p (q e) -> p q e", e=16))
        V.tensor_copy(b_tm[:, :, :], psB[:, 64:128].rearrange("p (q e) -> p q e", e=16))
        V.tensor_copy(msk[:, :, :], b_tm[:, :, :])
        V.memset(selm[:, :, :], 0.0)
        for r in range(4):
            for q in range(4):
                ctx.vedge(V.tensor_reduce(rcol[:, q, :], msk[:, q, :], AX.X, ALU.max))
                V.tensor_scalar(msk[:, q, :], msk[:, q, :], rcol[:, q, 0:1], None,
                                ALU.is_ge)
            V.tensor_add(selm[:, :, :], selm[:, :, :], msk[:, :, :])
            if r < 3:
                V.scalar_tensor_tensor(msk[:, :, :], selm[:, :, :], -1e30,
                                       b_tm[:, :, :], ALU.mult, ALU.add)
        V.tensor_scalar_min(selm[:, :, :], selm[:, :, :], 1.0)
        V.tensor_mul(selm[:, :, :], selm[:, :, :], s_tm[:, :, :])
        for q in range(4):
            ctx.vedge(V.tensor_reduce(rcol[:, q, :], selm[:, q, :], AX.X, ALU.add))
            ctx.vedge(V.reciprocal(rcol[:, q, :], rcol[:, q, :]))
            i = V.tensor_scalar(selm[:, q, :], selm[:, q, :], rcol[:, q, 0:1], 2.5,
                                ALU.mult, ALU.mult)
        i.then_inc(ve.h, 1)
        ve.inc()
        Tq.wait_ge(ve.h, ve.n)
        for q in range(4):
            i = Tq.transpose(psB[0:16, 256 + q * 128:256 + (q + 1) * 128],
                             selm[:, q, :], identf[:, :])
        i.then_inc(te.h, 1)
        te.inc()
        V.wait_ge(te.h, te.n)
        for q in range(4):
            i = V.tensor_copy(combT[:, q * 128:(q + 1) * 128],
                              psB[0:16, 256 + q * 128:256 + (q + 1) * 128])
        i.then_inc(ve.h, 1)
        ve.inc()
        if c == 0 and "dbg_sct" in io:
            Y.wait_ge(ve.h, ve.n)
            Y.dma_start(out=io["dbg_sct"], in_=sct[:, :, :]).then_inc(dy.h, 16)
            Y.dma_start(out=io["dbg_stm"], in_=s_tm[:, :, :]).then_inc(dy.h, 16)
            Y.dma_start(out=io["dbg_btm2"], in_=b_tm[:, :, :]).then_inc(dy.h, 16)
            Y.dma_start(out=io["dbg_selm"], in_=selm[:, :, :]).then_inc(dy.h, 16)
            Y.dma_start(out=io["dbg_combT"], in_=combT[:, :]).then_inc(dy.h, 16)
            dy.inc(80)
        V.wait_ge(dl.h, dl.n)
        V.memset(y0acc[:, :, :], 0.0).then_inc(ve.h, 1)
        ve.inc()
        for e in range(17):
            buf = e % 2
            if e + 1 < 17:
                Y.wait_ge(te.h, te.n)
                ldtoks[e + 1] = load_exp(e + 1, 1 - buf)
            if e < 16:
                Tq.wait_ge(ve.h, ve.n)
                Tq.matmul(psA[:, 512:512 + CH], sel16[:, e * 128:(e + 1) * 128],
                          combT[:, :], start=True, stop=True).then_inc(te.h, 1)
                te.inc()
                S.wait_ge(te.h, te.n)
                S.activation(bcomb[:, :], psA[:, 512:512 + CH],
                             AF.Identity).then_inc(se.h, 1)
                se.inc()
            Tq.wait_ge(dl.h, ldtoks[e])
            bkd = getattr(ctx, "banksD", None) or Banks()
            ctx.banksD = bkd
            sgtl = [sgt, ctx.sb("sgt2", [128, CH], F32)]
            for j in range(4):
                lo, hi = (0, CH) if j % 2 == 0 else (2 * CH, 3 * CH)
                bkd.wait(ctx, lo // CH)
                bkd.wait(ctx, hi // CH)
                for kj in range(8):
                    Tq.matmul(psB[:, lo:lo + CH], w13s_[:, buf, kj, j * 128:(j + 1) * 128],
                              xnb[:, kj, :], start=(kj == 0), stop=(kj == 7))
                for kj in range(8):
                    i = Tq.matmul(psB[:, hi:hi + CH],
                                  w13s_[:, buf, kj, (j + 4) * 128:(j + 5) * 128],
                                  xnb[:, kj, :], start=(kj == 0), stop=(kj == 7))
                i.then_inc(te.h, 1)
                te.inc()
                S.wait_ge(te.h, te.n)
                S.activation(sgtl[j % 2][:, :], psB[:, lo:lo + CH],
                             AF.Silu).then_inc(se.h, 1)
                se.inc()
                V.wait_ge(se.h, se.n)
                if e < 16:
                    V.tensor_mul(sgtl[j % 2][:, :], sgtl[j % 2][:, :], psB[:, hi:hi + CH])
                    V.tensor_mul(sgb[:, j, :], sgtl[j % 2][:, :],
                                 bcomb[:, :]).then_inc(ve.h, 1)
                else:
                    V.tensor_mul(sgb[:, j, :], sgtl[j % 2][:, :], psB[:, hi:hi + CH]
                                 ).then_inc(ve.h, 1)
                ve.inc()
                bkd.set(lo // CH, ve)
                bkd.set(hi // CH, ve)
            Tq.wait_ge(ve.h, ve.n)
            for o in range(8):
                bank = o % 4
                pdst = psB[:, bank * CH:(bank + 1) * CH]
                bkd.wait(ctx, bank)
                for kj in range(4):
                    i = Tq.matmul(pdst, w2s_[:, buf, kj, o * 128:(o + 1) * 128],
                                  sgb[:, kj, :], start=(kj == 0), stop=(kj == 3))
                i.then_inc(te.h, 1)
                te.inc()
                V.wait_ge(te.h, te.n)
                V.tensor_add(y0acc[:, o, :], y0acc[:, o, :], pdst).then_inc(ve.h, 1)
                ve.inc()
                bkd.set(bank, ve)
        S.wait_ge(ve.h, ve.n)
        S.activation(y0acc[:, :, :], y0acc[:, :, :], AF.Silu).then_inc(se.h, 1)
        se.inc()
        V.wait_ge(se.h, se.n)
        V.tensor_mul(y0acc[:, :, :], y0acc[:, :, :], s0buf[:, :, :])
        V.tensor_add(x1c[:, :, :], xrbuf[:, :, :], y0acc[:, :, :]).then_inc(ve.h, 1)
        ve.inc()
        Y.wait_ge(ve.h, ve.n)
        Y.dma_start(out=io["x1T"][:, tok].rearrange("(a p) t -> p a t", p=128),
                    in_=x1c[:, :, :]).then_inc(dy.h, 16)
        dy.inc(16)
    Y.wait_ge(dy.h, dy.n)


def build_phase_d(dbg=False):
    nc = bass.Bass("TRN2", target_bir_lowering=False)
    io = {}
    if dbg:
        for nm, shape in (("dbg_sct", [16, 4, CH]), ("dbg_stm", [128, 4, 16]),
                          ("dbg_btm2", [128, 4, 16]), ("dbg_selm", [128, 4, 16]),
                          ("dbg_combT", [16, CH])):
            io[nm] = nc.dram_tensor(nm, shape, F32, kind="ExternalOutput").ap()
    io["xn0Tb"] = nc.dram_tensor("xn0Tb", [D, T], BF16, kind="ExternalInput").ap()
    io["xn0T"] = nc.dram_tensor("xn0T", [D, T], F32, kind="ExternalInput").ap()
    io["xT"] = nc.dram_tensor("xT", [D, T], F32, kind="ExternalInput").ap()
    io["s0T"] = nc.dram_tensor("s0T", [D, T], F32, kind="ExternalInput").ap()
    io["gate0T"] = nc.dram_tensor("gate0T", [D, 16], F32, kind="ExternalInput").ap()
    io["gate0b"] = nc.dram_tensor("gate0b", [16, 1], F32, kind="ExternalInput").ap()
    io["sel16"] = nc.dram_tensor("sel16", [16, 16 * 128], F32, kind="ExternalInput").ap()
    io["identf"] = nc.dram_tensor("identf", [128, 128], F32, kind="ExternalInput").ap()
    io["w13b"] = nc.dram_tensor("w13b", [16, D, D], BF16, kind="ExternalInput").ap()
    io["w20b"] = nc.dram_tensor("w20b", [16, 512, D], BF16, kind="ExternalInput").ap()
    io["w13sb"] = nc.dram_tensor("w13sb", [D, D], BF16, kind="ExternalInput").ap()
    io["w2sb"] = nc.dram_tensor("w2sb", [512, D], BF16, kind="ExternalInput").ap()
    io["x1T"] = nc.dram_tensor("x1T", [D, T], F32, kind="ExternalOutput").ap()
    with contextlib.ExitStack() as stack:
        ctx = Ctx(nc, stack)
        emit_phase_d(ctx, io)
        ctx.finish()
    return nc


def emit_phase_f(ctx, io):
    Tq, V, S, G, Y = ctx.T, ctx.V, ctx.S, ctx.G, ctx.Y
    te, ve, se = ctx.sem("te"), ctx.sem("ve"), ctx.sem("se")
    dl, dy, gy = ctx.sem("dl"), ctx.sem("dy"), ctx.sem("gy")

    wq = ctx.sb("wq", [128, 8, D], BF16)
    wk = ctx.sb("wk", [128, 8, 64], BF16)
    wv = ctx.sb("wv", [128, 8, 64], BF16)
    bq8 = ctx.sb("bq8_s", [128, 8], F32)
    bkc = ctx.sb("bkc", [64, 1], F32)
    bvc = ctx.sb("bvc", [64, 1], F32)
    identb = ctx.sb("identb_s", [128, 128], BF16)
    xbuf = ctx.sb("xbuf", [128, 8, CH], F32)
    xnl = ctx.sb("xnl", [128, 8, CH], F32)
    xnlb = ctx.sb("xnlb_f", [128, 8, CH], BF16)
    kTe = ctx.sb("kTe", [65, T], BF16)
    vtme = ctx.sb("vtme", [128, 32, 65], BF16)
    qch = ctx.sb("qch", [65, 16, CH], BF16)
    q2 = ctx.sb("q2", [64, CH], F32)
    kn2 = ctx.sb("kn2", [1, CH], F32)
    kmax2 = ctx.sb("kmax2", [1, 2], F32)
    qn2 = ctx.sb("qn2", [1, 16, CH], F32)
    mrow = ctx.sb("mrow", [1, 16, CH], F32)
    ones64 = ctx.sb("ones64", [64, 1], F32)
    psA = ctx.ps("psA", [128, 2048], F32)
    psB = ctx.ps("psB", [128, 2048], F32)

    G.dma_start(out=wq[:, :, :],
                in_=io["Wq"].rearrange("(a p) o -> p a o", p=128)).then_inc(gy.h, 16)
    G.dma_start(out=wk[:, :, :],
                in_=io["Wk"].rearrange("(a p) o -> p a o", p=128)).then_inc(gy.h, 16)
    G.dma_start(out=wv[:, :, :],
                in_=io["Wv"].rearrange("(a p) o -> p a o", p=128)).then_inc(gy.h, 16)
    G.dma_start(out=identb[:, :], in_=io["identf"]).then_inc(gy.h, 16)
    gy.inc(64)
    Y.dma_start(out=bq8[:, :], in_=io["bq8"]).then_inc(dl.h, 16)
    Y.dma_start(out=bkc[:, :], in_=io["bk"]).then_inc(dl.h, 16)
    Y.dma_start(out=bvc[:, :], in_=io["bv"]).then_inc(dl.h, 16)
    dl.inc(48)
    init_common(ctx)
    V.wait_ge(dl.h, dl.n)
    V.wait_ge(gy.h, gy.n)
    V.memset(kTe[64:65, :], 1.0)
    V.memset(vtme[:, :, 64:65], 1.0)
    V.tensor_scalar_mul(bq8[:, :], bq8[:, :], 0.125)
    V.memset(ones64[:, :], 1.0)
    V.memset(kmax2[:, :], 0.0).then_inc(ve.h, 1)
    ve.inc()
    onesc = ctx.tiles["onesc"]

    for c in range(NCH):
        tok = slice(c * CH, (c + 1) * CH)
        Y.wait_ge(ve.h, ve.n)
        Y.wait_ge(se.h, se.n)
        Y.dma_start(out=xbuf[:, :, :],
                    in_=io["x1T"][:, tok].rearrange("(a p) t -> p a t", p=128)
                    ).then_inc(dl.h, 16)
        dl.inc(16)
        V.wait_ge(dy.h, dy.n)
        S.wait_ge(dy.h, dy.n)
        for e in (V, S, Tq):
            e.wait_ge(dl.h, dl.n)
        emit_layernorm_chunk(ctx, xbuf[:, :, :], 8, out_f32=xnl[:, :, :],
                             out_bf16=xnlb[:, :, :], name="lnl")
        Y.wait_ge(ve.h, ve.n)
        Y.dma_start(out=io["xnlT"][:, tok].rearrange("(a p) t -> p a t", p=128),
                    in_=xnl[:, :, :]).then_inc(dy.h, 16)
        Y.dma_start(out=io["xnlTb"][:, tok].rearrange("(a p) t -> p a t", p=128),
                    in_=xnlb[:, :, :]).then_inc(dy.h, 16)
        dy.inc(32)
        Tq.wait_ge(ve.h, ve.n)
        for kj in range(8):
            Tq.matmul(psB[0:64, 0:CH], wk[:, kj, :], xnlb[:, kj, :],
                      start=(kj == 0), stop=(kj == 7))
        for kj in range(8):
            i = Tq.matmul(psB[0:64, CH:2 * CH], wv[:, kj, :], xnlb[:, kj, :],
                          start=(kj == 0), stop=(kj == 7))
        i.then_inc(te.h, 1)
        te.inc()
        S.wait_ge(te.h, te.n)
        S.activation(kTe[0:64, tok], psB[0:64, 0:CH], AF.Identity, bias=bkc[:, 0:1])
        S.activation(qch[0:64, 0, :], psB[0:64, CH:2 * CH], AF.Identity,
                     bias=bvc[:, 0:1]).then_inc(se.h, 1)
        se.inc()
        S.activation(q2[:, :], psB[0:64, 0:CH], AF.Square).then_inc(se.h, 1)
        se.inc()
        V.wait_ge(se.h, se.n)
        V.tensor_copy(q2[:, :], kTe[0:64, tok]).then_inc(ve.h, 1)
        ve.inc()
        S.wait_ge(ve.h, ve.n)
        S.activation(q2[:, :], q2[:, :], AF.Square).then_inc(se.h, 1)
        se.inc()
        Tq.wait_ge(se.h, se.n)
        Tq.matmul(psB[0:1, 1024:1024 + CH], ones64[:, :], q2[:, :],
                  start=True, stop=True).then_inc(te.h, 1)
        te.inc()
        V.wait_ge(te.h, te.n)
        ctx.vedge(V.tensor_reduce(kmax2[0:1, 0:1], psB[0:1, 1024:1024 + CH],
                                  AX.X, ALU.max))
        ctx.vedge(V.tensor_max(kmax2[0:1, 1:2], kmax2[0:1, 1:2], kmax2[0:1, 0:1]))
        Tq.wait_ge(se.h, se.n)
        for q in range(4):
            i = Tq.transpose(psB[:, 1536 + q * 32:1536 + (q + 1) * 32].bitcast(BF16),
                             qch[0:64, 0, q * 128:(q + 1) * 128], identb[0:64, 0:64])
        i.then_inc(te.h, 1)
        te.inc()
        V.wait_ge(te.h, te.n)
        for q in range(4):
            i = V.tensor_copy(vtme[:, c * 4 + q, 0:64],
                              psB[:, 1536 + q * 32:1536 + (q + 1) * 32].bitcast(BF16))
        i.then_inc(ve.h, 1)
        ve.inc()
    Y.wait_ge(ve.h, ve.n)
    Y.wait_ge(se.h, se.n)
    Y.dma_start(out=io["kTeT"], in_=kTe[:, :]).then_inc(dy.h, 16)
    Y.dma_start(out=io["vtmeT"], in_=vtme[:, :, :]).then_inc(dy.h, 16)
    dy.inc(32)

    for c in range(NCH):
        tok = slice(c * CH, (c + 1) * CH)
        Y.wait_ge(ve.h, ve.n)
        Y.wait_ge(se.h, se.n)
        Y.dma_start(out=xnlb[:, :, :],
                    in_=io["xnlTb"][:, tok].rearrange("(a p) t -> p a t", p=128)
                    ).then_inc(dl.h, 16)
        dl.inc(16)
        S.wait_ge(dy.h, dy.n)
        V.wait_ge(dy.h, dy.n)
        Tq.wait_ge(dl.h, dl.n)
        Tq.wait_ge(se.h, se.n)
        for o in range(8):
            bank = o % 2
            pdst = psB[:, bank * CH:(bank + 1) * CH]
            Tq.wait_ge(se.h, se.n)
            for kj in range(8):
                i = Tq.matmul(pdst, wq[:, kj, o * 128:(o + 1) * 128], xnlb[:, kj, :],
                              start=(kj == 0), stop=(kj == 7))
            i.then_inc(te.h, 1)
            te.inc()
            S.wait_ge(te.h, te.n)
            S.activation(qch[0:64, 2 * o, :], pdst[0:64, :], AF.Identity,
                         scale=0.125, bias=bq8[0:64, o:o + 1])
            S.activation(qch[0:64, 2 * o + 1, :], pdst[64:128, :], AF.Identity,
                         scale=0.125, bias=bq8[64:128, o:o + 1]).then_inc(se.h, 1)
            se.inc()
        for h in range(16):
            S.wait_ge(se.h, se.n)
            S.activation(q2[:, :], qch[0:64, h, :], AF.Square).then_inc(se.h, 1)
            se.inc()
            Tq.wait_ge(se.h, se.n)
            Tq.matmul(psB[0:1, 1024 + (h % 2) * CH:1024 + (h % 2) * CH + CH],
                      ones64[:, :], q2[:, :], start=True, stop=True).then_inc(te.h, 1)
            te.inc()
            V.wait_ge(te.h, te.n)
            i = V.tensor_copy(qn2[0:1, h, :],
                              psB[0:1, 1024 + (h % 2) * CH:1024 + (h % 2) * CH + CH])
            i.then_inc(ve.h, 1)
            ve.inc()
        V.tensor_scalar(mrow[:, :, :], qn2[:, :, :], kmax2[0:1, 1:2], None,
                        ALU.mult).then_inc(ve.h, 1)
        ve.inc()
        S.wait_ge(ve.h, ve.n)
        S.activation(mrow[:, :, :], mrow[:, :, :], AF.Ln,
                     bias=ctx.tiles["eps1"][0:1, 0:1])
        S.activation(mrow[:, :, :], mrow[:, :, :], AF.Exp, scale=0.5).then_inc(se.h, 1)
        se.inc()
        V.wait_ge(se.h, se.n)
        V.tensor_scalar_mul(qch[64:65, :, :], mrow[:, :, :], -1.0).then_inc(ve.h, 1)
        ve.inc()
        Y.wait_ge(ve.h, ve.n)
        Y.dma_start(out=io["qTeT"][:, :, tok], in_=qch[:, :, :]).then_inc(dy.h, 16)
        dy.inc(16)
    Y.wait_ge(dy.h, dy.n)


def build_phase_f():
    nc = bass.Bass("TRN2", target_bir_lowering=False)
    io = {}
    io["x1T"] = nc.dram_tensor("x1T", [D, T], F32, kind="ExternalInput").ap()
    io["Wq"] = nc.dram_tensor("Wq", [D, D], F32, kind="ExternalInput").ap()
    io["bq8"] = nc.dram_tensor("bq8", [128, 8], F32, kind="ExternalInput").ap()
    io["Wk"] = nc.dram_tensor("Wk", [D, 64], F32, kind="ExternalInput").ap()
    io["bk"] = nc.dram_tensor("bk", [64, 1], F32, kind="ExternalInput").ap()
    io["Wv"] = nc.dram_tensor("Wv", [D, 64], F32, kind="ExternalInput").ap()
    io["bv"] = nc.dram_tensor("bv", [64, 1], F32, kind="ExternalInput").ap()
    io["identf"] = nc.dram_tensor("identf", [128, 128], F32, kind="ExternalInput").ap()
    io["xnlT"] = nc.dram_tensor("xnlT", [D, T], F32, kind="ExternalOutput").ap()
    io["xnlTb"] = nc.dram_tensor("xnlTb", [D, T], BF16, kind="ExternalOutput").ap()
    io["qTeT"] = nc.dram_tensor("qTeT", [65, 16, T], BF16, kind="ExternalOutput").ap()
    io["kTeT"] = nc.dram_tensor("kTeT", [65, T], BF16, kind="ExternalOutput").ap()
    io["vtmeT"] = nc.dram_tensor("vtmeT", [128, 32, 65], BF16, kind="ExternalOutput").ap()
    with contextlib.ExitStack() as stack:
        ctx = Ctx(nc, stack)
        emit_phase_f(ctx, io)
        ctx.finish()
    return nc


def emit_phase_g(ctx, io):
    Tq, V, S, G, Y = ctx.T, ctx.V, ctx.S, ctx.G, ctx.Y
    te, ve, se = ctx.sem("te"), ctx.sem("ve"), ctx.sem("se")
    dl, dy, gy = ctx.sem("dl"), ctx.sem("dy"), ctx.sem("gy")

    kTe = ctx.sb("kTe", [65, T], BF16)
    vtme = ctx.sb("vtme", [128, 32, 65], BF16)
    qch = ctx.sb("qch", [65, 16, CH], BF16)
    pT = ctx.sb("pT", [128, 2, 32, CH], BF16)
    rrow = ctx.sb("rrow", [1, CH], F32)
    rb = ctx.sb("rb", [128, CH], F32)
    atto = ctx.sb("atto", [128, 8, CH], BF16)
    wo = ctx.sb("wo", [128, 8, D], BF16)
    bo8 = ctx.sb("bo8_s", [128, 8], F32)
    s1c = ctx.sb("s1c", [128, 8, CH], F32)
    psA = ctx.ps("psA", [128, 2048], F32)
    psB = ctx.ps("psB", [128, 2048], F32)

    G.dma_start(out=wo[:, :, :],
                in_=io["Wo"].rearrange("(a p) o -> p a o", p=128)).then_inc(gy.h, 16)
    gy.inc(16)
    Y.dma_start(out=kTe[:, :], in_=io["kTeT"]).then_inc(dl.h, 16)
    Y.dma_start(out=vtme[:, :, :], in_=io["vtmeT"]).then_inc(dl.h, 16)
    Y.dma_start(out=bo8[:, :], in_=io["bo8"]).then_inc(dl.h, 16)
    dl.inc(48)
    init_common(ctx)
    V.wait_ge(dl.h, dl.n)
    ones1 = ctx.tiles["ones1"]
    Tq.wait_ge(gy.h, gy.n)
    Tq.wait_ge(dl.h, dl.n)

    for c in range(NCH):
        tok = slice(c * CH, (c + 1) * CH)
        Y.wait_ge(ve.h, ve.n)
        Y.wait_ge(se.h, se.n)
        Y.dma_start(out=qch[:, :, :], in_=io["qTeT"][:, :, tok]).then_inc(dl.h, 16)
        dl.inc(16)
        S.wait_ge(dy.h, dy.n)
        bkg = getattr(ctx, "banksG", None) or Banks()
        ctx.banksG = bkg
        pvready = {}
        attotok = {-1: 0}
        rbtok = {-1: 0}
        Tq.wait_ge(dl.h, dl.n)
        for hh in range(17):
            if hh < 16:
                h = hh
                for kt in range(32):
                    bank = kt % 4
                    bkg.wait(ctx, bank)
                    i = Tq.matmul(psB[:, bank * CH:(bank + 1) * CH],
                                  kTe[:, kt * 128:(kt + 1) * 128], qch[:, h, :],
                                  start=True, stop=True)
                    i.then_inc(te.h, 1)
                    te.inc()
                    S.wait_ge(te.h, te.n)
                    S.activation(pT[:, h % 2, kt, :],
                                 psB[:, bank * CH:(bank + 1) * CH],
                                 AF.Exp).then_inc(se.h, 1)
                    se.inc()
                    bkg.set(bank, se)
                pvready[h] = se.n
            if hh >= 1:
                h = hh - 1
                Tq.wait_ge(se.h, pvready[h])
                Tq.wait_ge(ve.h, attotok[h - 1])
                for kt in range(32):
                    i = Tq.matmul(psA[0:65, 0:CH], vtme[:, kt, :], pT[:, h % 2, kt, :],
                                  start=(kt == 0), stop=(kt == 31))
                i.then_inc(te.h, 1)
                te.inc()
                V.wait_ge(te.h, te.n)
                ctx.vedge(V.reciprocal(rrow[0:1, :], psA[64:65, 0:CH]))
                V.sem_inc(ve.h, 1)
                ve.inc()
                Tq.wait_ge(ve.h, ve.n)
                Tq.wait_ge(se.h, max(rbtok[h - 1], pvready[h]))
                Tq.matmul(psA[:, CH:2 * CH], ones1[:, :], rrow[0:1, :],
                          start=True, stop=True).then_inc(te.h, 1)
                te.inc()
                S.wait_ge(te.h, te.n)
                S.activation(rb[:, :], psA[:, CH:2 * CH], AF.Identity).then_inc(se.h, 1)
                rbtok[h] = se.inc()
                V.wait_ge(se.h, rbtok[h])
                V.tensor_mul(atto[(h % 2) * 64:(h % 2) * 64 + 64, h // 2, :],
                             psA[0:64, 0:CH], rb[0:64, :]).then_inc(ve.h, 1)
                attotok[h] = ve.inc()
        for o in range(8):
            bank = o % 2
            pdst = psB[:, bank * CH:(bank + 1) * CH]
            Tq.wait_ge(ve.h, ve.n)
            Tq.wait_ge(se.h, se.n)
            for kj in range(8):
                i = Tq.matmul(pdst, wo[:, kj, o * 128:(o + 1) * 128], atto[:, kj, :],
                              start=(kj == 0), stop=(kj == 7))
            i.then_inc(te.h, 1)
            te.inc()
            S.wait_ge(te.h, te.n)
            S.activation(s1c[:, o, :], pdst, AF.Identity,
                         bias=bo8[:, o:o + 1]).then_inc(se.h, 1)
            se.inc()
        Y.wait_ge(se.h, se.n)
        Y.dma_start(out=io["s1T"][:, tok].rearrange("(a p) t -> p a t", p=128),
                    in_=s1c[:, :, :]).then_inc(dy.h, 16)
        dy.inc(16)
    Y.wait_ge(dy.h, dy.n)


def build_phase_g(dbg=False):
    nc = bass.Bass("TRN2", target_bir_lowering=False)
    io = {}
    if dbg:
        io["dbg_pT"] = nc.dram_tensor("dbg_pT", [128, 32, CH], BF16,
                                      kind="ExternalOutput").ap()
        io["dbg_rrow"] = nc.dram_tensor("dbg_rrow", [1, CH], F32,
                                        kind="ExternalOutput").ap()
        io["dbg_rb"] = nc.dram_tensor("dbg_rb", [128, CH], F32,
                                      kind="ExternalOutput").ap()
        io["dbg_pv"] = nc.dram_tensor("dbg_pv", [65, CH], F32,
                                      kind="ExternalOutput").ap()
        io["dbg_atto"] = nc.dram_tensor("dbg_atto", [128, 8, CH], BF16,
                                        kind="ExternalOutput").ap()
    io["qTeT"] = nc.dram_tensor("qTeT", [65, 16, T], BF16, kind="ExternalInput").ap()
    io["kTeT"] = nc.dram_tensor("kTeT", [65, T], BF16, kind="ExternalInput").ap()
    io["vtmeT"] = nc.dram_tensor("vtmeT", [128, 32, 65], BF16, kind="ExternalInput").ap()
    io["Wo"] = nc.dram_tensor("Wo", [D, D], F32, kind="ExternalInput").ap()
    io["bo8"] = nc.dram_tensor("bo8", [128, 8], F32, kind="ExternalInput").ap()
    io["s1T"] = nc.dram_tensor("s1T", [D, T], F32, kind="ExternalOutput").ap()
    with contextlib.ExitStack() as stack:
        ctx = Ctx(nc, stack)
        emit_phase_g(ctx, io)
        ctx.finish()
    return nc


def emit_phase_h(ctx, io):
    Tq, V, S, G, Y = ctx.T, ctx.V, ctx.S, ctx.G, ctx.Y
    te, ve, se = ctx.sem("te"), ctx.sem("ve"), ctx.sem("se")
    dl, dy, gy = ctx.sem("dl"), ctx.sem("dy"), ctx.sem("gy")

    g1w = ctx.sb("g1w", [128, 8, 8], F32)
    sel8 = ctx.sb("sel8_s", [8, 8 * 128], F32)
    identf = ctx.sb("identf_s", [128, 128], F32)
    xnlb = ctx.sb("xnlb_f", [128, 8, CH], BF16)
    xnlf = ctx.sb("xnlf", [128, 8, CH], F32)
    x1c = ctx.sb("x1c", [128, 8, CH], F32)
    s1buf = ctx.sb("s1buf", [128, 8, CH], F32)
    y1acc = ctx.sb("y1acc", [128, 8, CH], F32)
    outc = ctx.sb("outc", [128, 8, CH], F32)
    zeb = ctx.sb("zeb", [128, 2, 8, CH], BF16)
    gT = ctx.sb("gT", [128, 22, CH], BF16)
    sgt = ctx.sb("sgt", [128, CH], F32)
    sgt2h = ctx.sb("sgt2h", [128, CH], F32)
    sgtl2 = [sgt, sgt2h]
    pb = ctx.sb("pb", [128, 2, CH], F32)
    w13sl = ctx.sb("w13sl", [128, 3, 8, 256], BF16)
    w2sl = ctx.sb("w2sl", [128, 3, 22, 128], BF16)
    lgr = ctx.sb("lgr", [8, CH], F32)
    lg_tm = ctx.sb("lg_tm", [128, 4, 8], F32)
    prb = ctx.sb("prb", [128, 4, 8], F32)
    selm = ctx.sb("selm8", [128, 4, 8], F32)
    mskh = ctx.sb("mskh", [128, 4, 8], F32)
    rcol = ctx.sb("rcol8", [128, 4, 1], F32)
    p8T = ctx.sb("p8T", [8, CH], F32)
    psA = ctx.ps("psA", [128, 2048], F32)
    psB = ctx.ps("psB", [128, 2048], F32)

    Y.dma_start(out=g1w[:, :, :],
                in_=io["gate1T"].rearrange("(a p) e -> p a e", p=128)).then_inc(dl.h, 16)
    Y.dma_start(out=sel8[:, :], in_=io["sel8"]).then_inc(dl.h, 16)
    Y.dma_start(out=identf[:, :], in_=io["identf"]).then_inc(dl.h, 16)
    dl.inc(48)
    init_common(ctx)
    V.wait_ge(dl.h, dl.n)
    id8 = identf[0:8, 0:8]

    def load_w13(e, o, buf):
        Y.dma_start(out=w13sl[:, buf, :, :],
                    in_=io["w13xb"][e, :, o * 256:(o + 1) * 256]
                    .rearrange("(a p) o -> p a o", p=128)).then_inc(dl.h, 16)
        return dl.inc(16)

    def load_w2(e, o2, buf):
        Y.dma_start(out=w2sl[:, buf, :, :],
                    in_=io["w21b"][e, :, o2 * 128:(o2 + 1) * 128]
                    .rearrange("(a p) o -> p a o", p=128)).then_inc(dl.h, 16)
        return dl.inc(16)

    for c in range(NCH):
        tok = slice(c * CH, (c + 1) * CH)
        Y.wait_ge(ve.h, ve.n)
        Y.wait_ge(se.h, se.n)
        Y.dma_start(out=xnlb[:, :, :],
                    in_=io["xnlTb"][:, tok].rearrange("(a p) t -> p a t", p=128)
                    ).then_inc(dl.h, 16)
        Y.dma_start(out=xnlf[:, :, :],
                    in_=io["xnlT"][:, tok].rearrange("(a p) t -> p a t", p=128)
                    ).then_inc(dl.h, 16)
        Y.dma_start(out=x1c[:, :, :],
                    in_=io["x1T"][:, tok].rearrange("(a p) t -> p a t", p=128)
                    ).then_inc(dl.h, 16)
        Y.dma_start(out=s1buf[:, :, :],
                    in_=io["s1T"][:, tok].rearrange("(a p) t -> p a t", p=128)
                    ).then_inc(dl.h, 16)
        dl.inc(64)
        V.wait_ge(dy.h, dy.n)
        Tq.wait_ge(dl.h, dl.n)
        Tq.wait_ge(ve.h, ve.n)
        for kj in range(8):
            i = Tq.matmul(psA[0:8, 0:CH], g1w[:, kj, :], xnlf[:, kj, :],
                          start=(kj == 0), stop=(kj == 7))
        i.then_inc(te.h, 1)
        te.inc()
        V.wait_ge(te.h, te.n)
        V.tensor_copy(lgr[:, :], psA[0:8, 0:CH]).then_inc(ve.h, 1)
        ve.inc()
        Tq.wait_ge(ve.h, ve.n)
        for q in range(4):
            i = Tq.transpose(psB[0:128, q * 8:(q + 1) * 8],
                             lgr[:, q * 128:(q + 1) * 128], id8)
        i.then_inc(te.h, 1)
        te.inc()
        V.wait_ge(te.h, te.n)
        V.tensor_copy(lg_tm[:, :, :], psB[:, 0:32].rearrange("p (q e) -> p q e", e=8)
                      ).then_inc(ve.h, 1)
        ve.inc()
        for q in range(4):
            ctx.vedge(V.tensor_reduce(rcol[:, q, :], lg_tm[:, q, :], AX.X, ALU.max))
            V.tensor_scalar(lg_tm[:, q, :], lg_tm[:, q, :], rcol[:, q, 0:1], None,
                            ALU.subtract)
        V.sem_inc(ve.h, 1)
        ve.inc()
        S.wait_ge(ve.h, ve.n)
        S.activation(prb[:, :, :], lg_tm[:, :, :], AF.Exp).then_inc(se.h, 1)
        se.inc()
        V.wait_ge(se.h, se.n)
        for q in range(4):
            ctx.vedge(V.tensor_reduce(rcol[:, q, :], prb[:, q, :], AX.X, ALU.add))
            ctx.vedge(V.reciprocal(rcol[:, q, :], rcol[:, q, :]))
            V.tensor_scalar(prb[:, q, :], prb[:, q, :], rcol[:, q, 0:1], None,
                            ALU.mult)
        V.sem_inc(ve.h, 1)
        ve.inc()
        V.tensor_copy(mskh[:, :, :], prb[:, :, :])
        V.memset(selm[:, :, :], 0.0)
        for r in range(2):
            for q in range(4):
                ctx.vedge(V.tensor_reduce(rcol[:, q, :], mskh[:, q, :], AX.X, ALU.max))
                V.tensor_scalar(mskh[:, q, :], mskh[:, q, :], rcol[:, q, 0:1], None,
                                ALU.is_ge)
            V.tensor_add(selm[:, :, :], selm[:, :, :], mskh[:, :, :])
            if r < 1:
                V.scalar_tensor_tensor(mskh[:, :, :], selm[:, :, :], -2.0,
                                       prb[:, :, :], ALU.mult, ALU.add)
        V.tensor_scalar_min(selm[:, :, :], selm[:, :, :], 1.0)
        V.tensor_mul(selm[:, :, :], selm[:, :, :], prb[:, :, :]).then_inc(ve.h, 1)
        ve.inc()
        Tq.wait_ge(ve.h, ve.n)
        for q in range(4):
            i = Tq.transpose(psB[0:8, 256 + q * 128:256 + (q + 1) * 128],
                             selm[:, q, :], identf[:, :])
        i.then_inc(te.h, 1)
        te.inc()
        V.wait_ge(te.h, te.n)
        for q in range(4):
            i = V.tensor_copy(p8T[:, q * 128:(q + 1) * 128],
                              psB[0:8, 256 + q * 128:256 + (q + 1) * 128])
        i.then_inc(ve.h, 1)
        ve.inc()
        V.memset(y1acc[:, :, :], 0.0).then_inc(ve.h, 1)
        ve.inc()
        bkh = getattr(ctx, "banksH", None) or Banks()
        ctx.banksH = bkh
        lw13 = {}
        lw2 = {}
        lw13tail = {}
        lw2tail = {}
        zebtok = {}

        def emit_zeb(e):
            Tq.wait_ge(se.h, pbtok.get(e - 1, 0))
            Tq.matmul(psA[:, 512:512 + CH], sel8[:, e * 128:(e + 1) * 128],
                      p8T[:, :], start=True, stop=True).then_inc(te.h, 1)
            te.inc()
            S.wait_ge(te.h, te.n)
            S.activation(pb[:, e % 2, :], psA[:, 512:512 + CH],
                         AF.Identity).then_inc(se.h, 1)
            pbtok[e] = se.inc()
            V.wait_ge(se.h, pbtok[e])
            V.tensor_tensor(zeb[:, e % 2, :, :], xnlb[:, :, :],
                            pb[:, e % 2, :].unsqueeze(1).to_broadcast([128, 8, CH]),
                            ALU.mult).then_inc(ve.h, 1)
            zebtok[e] = ve.inc()

        pbtok = {}
        emit_zeb(0)
        for q13 in range(2):
            lw13[(0, q13)] = load_w13(0, q13, q13)
        for e in range(8):
            if e + 1 < 8:
                emit_zeb(e + 1)
            Tq.wait_ge(ve.h, zebtok[e])
            for o in range(22):
                gsl = 22 * e + o
                buf = gsl % 3
                k2 = gsl + 2
                nxt = (e, o + 2) if o + 2 < 22 else (
                    (e + 1, o + 2 - 22) if e + 1 < 8 else None)
                if nxt is not None and nxt not in lw13:
                    b2l = k2 % 3
                    Y.wait_ge(te.h, lw13tail.get(b2l, 0))
                    lw13[nxt] = load_w13(nxt[0], nxt[1], b2l)
                lo, hi = (0, CH) if o % 2 == 0 else (2 * CH, 3 * CH)
                Tq.wait_ge(dl.h, lw13[(e, o)])
                bkh.wait(ctx, lo // CH)
                bkh.wait(ctx, hi // CH)
                for kj in range(8):
                    Tq.matmul(psB[:, lo:lo + CH], w13sl[:, buf, kj, 0:128],
                              zeb[:, e % 2, kj, :], start=(kj == 0), stop=(kj == 7))
                for kj in range(8):
                    i = Tq.matmul(psB[:, hi:hi + CH], w13sl[:, buf, kj, 128:256],
                                  zeb[:, e % 2, kj, :], start=(kj == 0), stop=(kj == 7))
                i.then_inc(te.h, 1)
                lw13tail[buf] = te.inc()
                S.wait_ge(te.h, te.n)
                S.activation(sgtl2[o % 2][:, :], psB[:, lo:lo + CH],
                             AF.Silu).then_inc(se.h, 1)
                se.inc()
                V.wait_ge(se.h, se.n)
                V.tensor_mul(gT[:, o, :], sgtl2[o % 2][:, :],
                             psB[:, hi:hi + CH]).then_inc(ve.h, 1)
                ve.inc()
                bkh.set(lo // CH, ve)
                bkh.set(hi // CH, ve)
            for q2 in range(2):
                Y.wait_ge(te.h, lw2tail.get(q2, 0))
                lw2[(e, q2)] = load_w2(e, q2, q2)
            Tq.wait_ge(ve.h, ve.n)
            for o2 in range(8):
                buf = o2 % 3
                if o2 + 2 < 8:
                    b2l = (o2 + 2) % 3
                    Y.wait_ge(te.h, lw2tail.get(b2l, 0))
                    lw2[(e, o2 + 2)] = load_w2(e, o2 + 2, b2l)
                Tq.wait_ge(dl.h, lw2[(e, o2)])
                bank = o2 % 4
                pdst = psB[:, bank * CH:(bank + 1) * CH]
                bkh.wait(ctx, bank)
                for kj in range(22):
                    i = Tq.matmul(pdst, w2sl[:, buf, kj, :], gT[:, kj, :],
                                  start=(kj == 0), stop=(kj == 21))
                i.then_inc(te.h, 1)
                lw2tail[buf] = te.inc()
                V.wait_ge(te.h, te.n)
                V.tensor_add(y1acc[:, o2, :], y1acc[:, o2, :], pdst).then_inc(ve.h, 1)
                ve.inc()
                bkh.set(bank, ve)
        S.wait_ge(ve.h, ve.n)
        S.activation(y1acc[:, :, :], y1acc[:, :, :], AF.Silu).then_inc(se.h, 1)
        se.inc()
        V.wait_ge(se.h, se.n)
        V.tensor_mul(y1acc[:, :, :], y1acc[:, :, :], s1buf[:, :, :])
        V.tensor_add(outc[:, :, :], x1c[:, :, :], y1acc[:, :, :]).then_inc(ve.h, 1)
        ve.inc()
        Y.wait_ge(ve.h, ve.n)
        Y.dma_start(out=io["outT"][:, tok].rearrange("(a p) t -> p a t", p=128),
                    in_=outc[:, :, :]).then_inc(dy.h, 16)
        dy.inc(16)
    Y.wait_ge(dy.h, dy.n)


def build_phase_h():
    nc = bass.Bass("TRN2", target_bir_lowering=False)
    io = {}
    io["xnlTb"] = nc.dram_tensor("xnlTb", [D, T], BF16, kind="ExternalInput").ap()
    io["xnlT"] = nc.dram_tensor("xnlT", [D, T], F32, kind="ExternalInput").ap()
    io["x1T"] = nc.dram_tensor("x1T", [D, T], F32, kind="ExternalInput").ap()
    io["s1T"] = nc.dram_tensor("s1T", [D, T], F32, kind="ExternalInput").ap()
    io["gate1T"] = nc.dram_tensor("gate1T", [D, 8], F32, kind="ExternalInput").ap()
    io["sel8"] = nc.dram_tensor("sel8", [8, 8 * 128], F32, kind="ExternalInput").ap()
    io["identf"] = nc.dram_tensor("identf", [128, 128], F32, kind="ExternalInput").ap()
    io["w13xb"] = nc.dram_tensor("w13xb", [8, D, 5632], BF16, kind="ExternalInput").ap()
    io["w21b"] = nc.dram_tensor("w21b", [8, 2816, D], BF16, kind="ExternalInput").ap()
    io["outT"] = nc.dram_tensor("outT", [D, T], F32, kind="ExternalOutput").ap()
    with contextlib.ExitStack() as stack:
        ctx = Ctx(nc, stack)
        emit_phase_h(ctx, io)
        ctx.finish()
    return nc


def build_phase_w():
    nc = bass.Bass("TRN2", target_bir_lowering=False)
    specs = [
        ("w13T", [16, D, D], "w13b"),
        ("w20T", [16, 512, D], "w20b"),
        ("w13sT", [1, D, D], "w13sb2"),
        ("w2sT", [1, 512, D], "w2sb2"),
        ("w13x", [8, D, 5632], "w13xb"),
        ("w21T", [8, 2816, D], "w21b"),
    ]
    with contextlib.ExitStack() as stack:
        sem = stack.enter_context(nc.semaphore("cast"))
        block = stack.enter_context(nc.Block())
        n = 0
        calls = []
        for nm_in, shape, nm_out in specs:
            src = nc.dram_tensor(nm_in, shape, F32, kind="ExternalInput").ap()
            dst = nc.dram_tensor(nm_out, shape, BF16, kind="ExternalOutput").ap()
            for e in range(shape[0]):
                calls.append((dst[e], src[e]))
                n += 16
        def body(g):
            for dstap, srcap in calls:
                g.dma_start(out=dstap, in_=srcap).then_inc(sem, 16)
            g.wait_ge(sem, n)
        block.gpsimd(body)
    return nc


def emit_phase_w(ctx, pairs):
    G = ctx.G
    gy = ctx.sem("gy")
    for src, dst, n in pairs:
        for e in range(n):
            G.dma_start(out=dst[e] if n > 1 else dst,
                        in_=src[e] if n > 1 else src).then_inc(gy.h, 16)
            gy.inc(16)
    G.wait_ge(gy.h, gy.n)

def build_fused():
    nc = bass.Bass("TRN2", target_bir_lowering=False)
    I, O, N = "ExternalInput", "ExternalOutput", "Internal"
    t = {}
    def dram(nm, shape, dt_, kind):
        t[nm] = nc.dram_tensor(nm, list(shape), dt_, kind=kind).ap()
        return t[nm]
    dram("xT", [D, T], F32, I)
    for nm, shape in (("inprojT", [D, DPROJ]), ("lnw8", [128, 8]), ("lnb8", [128, 8]),
                      ("convw", [CONVD, 4]), ("convb", [128, 18]), ("dtbias", [32, 1]),
                      ("alog", [32, 1]), ("drep", [128, 16]), ("normw16", [128, 16]),
                      ("outprojT", [DI, D]), ("gate0T", [D, 16]), ("gate0b", [16, 1]),
                      ("w13T", [16, D, D]), ("w20T", [16, 512, D]),
                      ("w13sT", [1, D, D]), ("w2sT", [1, 512, D]),
                      ("Wq", [D, D]), ("bq8", [128, 8]), ("Wk", [D, 64]),
                      ("bk", [64, 1]), ("Wv", [D, 64]), ("bv", [64, 1]),
                      ("Wo", [D, D]), ("bo8", [128, 8]), ("gate1T", [D, 8]),
                      ("w13x", [8, D, 5632]), ("w21T", [8, 2816, D]),
                      ("identf", [128, 128]), ("maskneg", [128, 128]),
                      ("rowsel32", [32, NH * 128]), ("sel16", [16, 16 * 128]),
                      ("sel8", [8, 8 * 128])):
        dram(nm, shape, F32, I)
    for nm, shape, dt_ in (("xn0T", [D, T], F32), ("xn0Tb", [D, T], BF16),
                           ("szT", [DI, T], F32), ("convoT", [CONVD, T], BF16),
                           ("dtT", [32, T], F32), ("yT", [DI, T], F32),
                           ("s0T", [D, T], F32), ("x1T", [D, T], F32),
                           ("xnlT", [D, T], F32), ("xnlTb", [D, T], BF16),
                           ("qTeT", [65, 16, T], BF16), ("kTeT", [65, T], BF16),
                           ("vtmeT", [128, 32, 65], BF16), ("s1T", [D, T], F32),
                           ("w13b", [16, D, D], BF16), ("w20b", [16, 512, D], BF16),
                           ("w13sb2", [1, D, D], BF16), ("w2sb2", [1, 512, D], BF16),
                           ("w13xb", [8, D, 5632], BF16), ("w21b", [8, 2816, D], BF16)):
        dram(nm, shape, dt_, N)
    dram("outT", [D, T], F32, O)

    io = dict(t)
    io["w13sb"] = t["w13sb2"][0]
    io["w2sb"] = t["w2sb2"][0]

    with contextlib.ExitStack() as outer:
        ctx = Ctx(nc, outer)
        init_common(ctx)
        ctx.ps("psA", [128, 2048], F32)
        ctx.ps("psB", [128, 2048], F32)
        block = outer.enter_context(nc.Block())
        phases = [
            (lambda: emit_phase_a(ctx, io), False),
            (lambda: emit_phase_w(ctx, [
                (t["w13T"], t["w13b"], 16), (t["w20T"], t["w20b"], 16),
                (t["w13sT"], t["w13sb2"], 1), (t["w2sT"], t["w2sb2"], 1),
                (t["w13x"], t["w13xb"], 8), (t["w21T"], t["w21b"], 8)]), False),
            (lambda: emit_phase_b(ctx, io), True),
            (lambda: emit_phase_c(ctx, io), True),
            (lambda: emit_phase_d(ctx, io), True),
            (lambda: emit_phase_f(ctx, io), True),
            (lambda: emit_phase_g(ctx, io), True),
            (lambda: emit_phase_h(ctx, io), True),
        ]
        for pi, (phase, bar) in enumerate(phases):
            ctx.nphase = pi
            with contextlib.ExitStack() as scope:
                ctx.scope = scope
                ctx.scope_tiles = {}
                if bar:
                    barrier(ctx)
                phase()
                ctx.replay_into(block)
            ctx.scope = None
    return nc

def prep_fused_inputs(xT_core, w, consts):
    m = {"xT": xT_core}
    for k in ("inprojT", "lnw8", "lnb8", "convw", "convb", "dtbias", "alog", "drep",
              "normw16", "outprojT", "gate0T", "gate0b", "w13T", "w20T", "w13sT",
              "w2sT", "Wq", "bq8", "Wk", "bk", "Wv", "bv", "Wo", "bo8", "gate1T",
              "w13x", "w21T"):
        m[k] = w[k]
    for k in ("identf", "maskneg", "rowsel32", "sel16", "sel8"):
        m[k] = consts[k]
    return m


def _col8(v):
    return np.ascontiguousarray(v.reshape(8, 128).T)

def prep_phase_a_inputs(xT_core, w):
    return {
        "xT": xT_core,
        "inprojT": w["inprojT"],
        "lnw8": w["lnw8"], "lnb8": w["lnb8"],
        "convw": w["convw"], "convb": w["convb"],
        "dtbias": w["dtbias"],
    }

def build_phase_a():
    nc = bass.Bass("TRN2", target_bir_lowering=False)
    io = {}
    io["xT"] = nc.dram_tensor("xT", [D, T], F32, kind="ExternalInput").ap()
    io["inprojT"] = nc.dram_tensor("inprojT", [D, DPROJ], F32, kind="ExternalInput").ap()
    io["lnw8"] = nc.dram_tensor("lnw8", [128, 8], F32, kind="ExternalInput").ap()
    io["lnb8"] = nc.dram_tensor("lnb8", [128, 8], F32, kind="ExternalInput").ap()
    io["convw"] = nc.dram_tensor("convw", [CONVD, 4], F32, kind="ExternalInput").ap()
    io["convb"] = nc.dram_tensor("convb", [128, 18], F32, kind="ExternalInput").ap()
    io["dtbias"] = nc.dram_tensor("dtbias", [32, 1], F32, kind="ExternalInput").ap()
    io["xn0T"] = nc.dram_tensor("xn0T", [D, T], F32, kind="ExternalOutput").ap()
    io["xn0Tb"] = nc.dram_tensor("xn0Tb", [D, T], BF16, kind="ExternalOutput").ap()
    io["szT"] = nc.dram_tensor("szT", [DI, T], F32, kind="ExternalOutput").ap()
    io["convoT"] = nc.dram_tensor("convoT", [CONVD, T], BF16, kind="ExternalOutput").ap()
    io["dtT"] = nc.dram_tensor("dtT", [32, T], F32, kind="ExternalOutput").ap()
    with contextlib.ExitStack() as stack:
        ctx = Ctx(nc, stack)
        emit_phase_a(ctx, io)
        ctx.finish()
    return nc

def prep_weights(inp):
    C = np.ascontiguousarray
    w = {}
    w["inprojT"] = C(inp["L0_attn_in_proj_w"].T)
    w["lnw8"] = _col8(inp["L0_attn_ln_w"])
    w["lnb8"] = _col8(inp["L0_attn_ln_b"])
    w["convw"] = C(inp["L0_attn_conv_w"])
    w["convb"] = C(inp["L0_attn_conv_b"].reshape(18, 128).T)
    w["dtbias"] = C(inp["L0_attn_dt_bias"].reshape(32, 1))
    w["alog"] = C(inp["L0_attn_A_log"].reshape(32, 1))
    w["drep"] = C(np.repeat(inp["L0_attn_D"], 64).reshape(16, 128).T)
    w["normw16"] = C(inp["L0_attn_norm_w"].reshape(16, 128).T)
    w["outprojT"] = C(inp["L0_attn_out_proj_w"].T)
    w["gate0T"] = C(inp["L0_ffn_gate_w"].T)
    w["gate0b"] = C(inp["L0_ffn_gate_b"].reshape(16, 1))
    w["w13T"] = C(inp["L0_ffn_w13"].transpose(0, 2, 1))
    w["w20T"] = C(inp["L0_ffn_w2"].transpose(0, 2, 1))
    w["w13sT"] = C(inp["L0_ffn_w13_s"].T.reshape(1, D, D))
    w["w2sT"] = C(inp["L0_ffn_w2_s"].T.reshape(1, 512, D))
    w["Wq"] = C(inp["L1_attn_Wq"])
    w["bq8"] = _col8(inp["L1_attn_bq"])
    w["Wk"] = C(inp["L1_attn_Wk"])
    w["bk"] = C(inp["L1_attn_bk"].reshape(64, 1))
    w["Wv"] = C(inp["L1_attn_Wv"])
    w["bv"] = C(inp["L1_attn_bv"].reshape(64, 1))
    w["Wo"] = C(inp["L1_attn_Wo"])
    w["bo8"] = _col8(inp["L1_attn_bo"])
    w["gate1T"] = C(inp["L1_ffn_gate_w"].T)
    w1 = inp["L1_ffn_w1"].reshape(8, D, 22, 128)
    w3 = inp["L1_ffn_w3"].reshape(8, D, 22, 128)
    w["w13x"] = C(np.concatenate([w1, w3], axis=3).reshape(8, D, 5632))
    w["w21T"] = C(inp["L1_ffn_w2"])
    return w

CORES = list(range(8))

def _run(nc, in_maps):
    return run_bass_kernel_spmd(nc, in_maps, core_ids=CORES).results

def kernel(**inputs):
    inputs = {k: np.asarray(v) for k, v in inputs.items()}
    _ = inputs["L0_ffn_input_ids"]
    x = inputs["L0_attn_x"]
    w = prep_weights(inputs)
    consts = prep_consts()
    xTs = [np.ascontiguousarray(x[b].T) for b in range(8)]
    nc = build_fused()
    res = run_bass_kernel_spmd(
        nc, [prep_fused_inputs(xTs[b], w, consts) for b in CORES], core_ids=CORES)
    return np.ascontiguousarray(
        np.stack([r["outT"].T for r in res.results]).astype(np.float32))

def kernel_unfused(**inputs):
    inputs = {k: np.asarray(v) for k, v in inputs.items()}
    x = inputs["L0_attn_x"]
    w = prep_weights(inputs)
    consts = prep_consts()
    xTs = [np.ascontiguousarray(x[b].T) for b in range(8)]

    rW = _run(build_phase_w(), [{
        "w13T": w["w13T"], "w20T": w["w20T"], "w13sT": w["w13sT"], "w2sT": w["w2sT"],
        "w13x": w["w13x"], "w21T": w["w21T"]} for _ in CORES])
    rA = _run(build_phase_a(), [prep_phase_a_inputs(xTs[b], w) for b in CORES])
    rB = _run(build_phase_b(), [{
        "convoT": rA[b]["convoT"], "dtT": rA[b]["dtT"], "alog": w["alog"],
        "drep": w["drep"], "identf": consts["identf"], "maskneg": consts["maskneg"],
        "rowsel32": consts["rowsel32"]} for b in CORES])
    rC = _run(build_phase_c(), [{
        "yT": rB[b]["yT"], "szT": rA[b]["szT"], "xn0T": rA[b]["xn0T"],
        "outprojT": w["outprojT"], "normw16": w["normw16"]} for b in CORES])
    rD = _run(build_phase_d(), [{
        "xn0Tb": rA[b]["xn0Tb"], "xn0T": rA[b]["xn0T"], "xT": xTs[b],
        "s0T": rC[b]["s0T"], "gate0T": w["gate0T"], "gate0b": w["gate0b"],
        "sel16": consts["sel16"], "identf": consts["identf"],
        "w13b": rW[b]["w13b"], "w20b": rW[b]["w20b"],
        "w13sb": rW[b]["w13sb2"].reshape(D, D),
        "w2sb": rW[b]["w2sb2"].reshape(512, D)} for b in CORES])
    rF = _run(build_phase_f(), [{
        "x1T": rD[b]["x1T"], "Wq": w["Wq"], "bq8": w["bq8"], "Wk": w["Wk"],
        "bk": w["bk"], "Wv": w["Wv"], "bv": w["bv"],
        "identf": consts["identf"]} for b in CORES])
    rG = _run(build_phase_g(), [{
        "qTeT": rF[b]["qTeT"], "kTeT": rF[b]["kTeT"], "vtmeT": rF[b]["vtmeT"],
        "Wo": w["Wo"], "bo8": w["bo8"]} for b in CORES])
    rH = _run(build_phase_h(), [{
        "xnlTb": rF[b]["xnlTb"], "xnlT": rF[b]["xnlT"], "x1T": rD[b]["x1T"],
        "s1T": rG[b]["s1T"], "gate1T": w["gate1T"], "sel8": consts["sel8"],
        "identf": consts["identf"], "w13xb": rW[b]["w13xb"],
        "w21b": rW[b]["w21b"]} for b in CORES])
    return np.ascontiguousarray(
        np.stack([rH[b]["outT"].T for b in CORES]).astype(np.float32))
